# version 08 (final)
# speedup vs baseline: 1.6428x; 1.0134x over previous
import numpy as np
import concourse.bass as bass
import concourse.mybir as mybir
from concourse.bass_utils import run_bass_kernel_spmd
from contextlib import ExitStack

F32 = mybir.dt.float32
BF16 = mybir.dt.bfloat16
I32 = mybir.dt.int32
U32 = mybir.dt.uint32
AF = mybir.ActivationFunctionType
ALU = mybir.AluOpType
AX = mybir.AxisListType

D = 1024
T = 4096
TL = 2048
NP = 128
NH = 16
DH = 64
LAT = 512
EPS = 1e-5


class Ctr:
    def __init__(self):
        self.v = 0

    def add(self, k=1):
        self.v += k
        return self.v



def build_l0_attn(h, debug=False):
    nc = bass.Bass("TRN2", target_bir_lowering=False)
    Q0 = h * TL
    SMAX = Q0 + TL
    NSL = SMAX // 512
    NMAC = TL // 512

    xT = nc.dram_tensor("xT", [D, SMAX], F32, kind="ExternalInput").ap()
    wcd = nc.dram_tensor("wcd", [D, LAT], F32, kind="ExternalInput").ap()
    wqu = nc.dram_tensor("wqu", [LAT, D], F32, kind="ExternalInput").ap()
    wkv = nc.dram_tensor("wkv", [LAT, 2 * DH], F32, kind="ExternalInput").ap()
    wcp = nc.dram_tensor("wcp", [D, D], F32, kind="ExternalInput").ap()
    x1 = nc.dram_tensor("x1", [D, TL], F32, kind="ExternalOutput").ap()
    if debug:
        SMAXl = h * TL + TL
        dbg_lat = nc.dram_tensor("dbg_lat", [NP, 4 * SMAXl], BF16, kind="ExternalOutput").ap()
        dbg_k2 = nc.dram_tensor("dbg_k2", [NP, SMAXl], BF16, kind="ExternalOutput").ap()
        dbg_vo = nc.dram_tensor("dbg_vo", [NP, 65 * (SMAXl // NP)], BF16, kind="ExternalOutput").ap()
        dbg_qm = nc.dram_tensor("dbg_qm", [NP, 8 * 512], BF16, kind="ExternalOutput").ap()
        dbg_cs = nc.dram_tensor("dbg_cs", [NP, 8 * 512], BF16, kind="ExternalOutput").ap()
        dbg_sm = nc.dram_tensor("dbg_sm", [NP, 8], F32, kind="ExternalOutput").ap()
        dbg_xp = nc.dram_tensor("dbg_xp", [NP, 8 * 512], F32, kind="ExternalOutput").ap()
        dbg_ab = nc.dram_tensor("dbg_ab", [1, 1024], F32, kind="ExternalOutput").ap()

    with ExitStack() as ctx:
        E = ctx.enter_context
        xown = E(nc.sbuf_tensor("xown", [NP, 8 * TL], F32))
        lat = E(nc.sbuf_tensor("lat", [NP, 4 * SMAX], BF16))
        k2 = E(nc.sbuf_tensor("k2", [NP, SMAX], BF16))
        vo = E(nc.sbuf_tensor("vo", [NP, 65 * (SMAX // NP)], BF16))
        wcd_b = E(nc.sbuf_tensor("wcd_b", [NP, 8 * LAT], BF16))
        wqu_b = E(nc.sbuf_tensor("wqu_b", [NP, 4 * D], BF16))
        wkv_b = E(nc.sbuf_tensor("wkv_b", [NP, 4 * 2 * DH], BF16))
        ones_p1 = E(nc.sbuf_tensor("ones_p1", [NP, 1], F32))
        ones_1p = E(nc.sbuf_tensor("ones_1p", [1, NP], F32))
        sc125 = E(nc.sbuf_tensor("sc125", [NP, 1], F32))
        epst = E(nc.sbuf_tensor("epst", [1, 1], F32))
        mtri = E(nc.sbuf_tensor("mtri", [NP, NP], BF16))
        id64 = E(nc.sbuf_tensor("id64", [DH, DH], BF16))
        xsq = E(nc.sbuf_tensor("xsq", [NP, 2 * 512], F32))
        xln = E(nc.sbuf_tensor("xln", [NP, 8 * 512], BF16))
        stat = E(nc.sbuf_tensor("stat", [1, 4 * 512], F32))
        kvt = E(nc.sbuf_tensor("kvt", [DH, 512], BF16))
        ksq = E(nc.sbuf_tensor("ksq", [DH, 512], F32))
        kmax2 = E(nc.sbuf_tensor("kmax2", [1, 2], F32))
        qm = E(nc.sbuf_tensor("qm", [NP, 8 * 512], BF16))
        qsq = E(nc.sbuf_tensor("qsq", [NP, 512], F32))
        qmax2 = E(nc.sbuf_tensor("qmax2", [1, 2], F32))
        biasm = E(nc.sbuf_tensor("biasm", [NP, 1], F32))
        att = E(nc.sbuf_tensor("att", [NP, 4 * 1024], BF16))
        cs = E(nc.sbuf_tensor("cs", [NP, 8 * 512], BF16))
        sb11 = E(nc.sbuf_tensor("sb11", [1, 8], F32))

        s_ld = E(nc.semaphore("s_ld"))
        s_l0 = E(nc.semaphore("s_l0"))
        s_l1 = E(nc.semaphore("s_l1"))
        s_gw = E(nc.semaphore("s_gw"))
        s_gw2 = E(nc.semaphore("s_gw2"))
        s_pe = E(nc.semaphore("s_pe"))
        s_se = E(nc.semaphore("s_se"))
        s_ve = E(nc.semaphore("s_ve"))
        s_gp = E(nc.semaphore("s_gp"))
        s_st = E(nc.semaphore("s_st"))
        block = E(nc.Block())

        c_ld, c_gw, c_pe, c_se, c_ve, c_gp, c_st = (Ctr() for _ in range(7))

        @block.sync
        def _(sync):
            for c in range(8):
                sync.dma_start(out=xown[:, TL * c:TL * (c + 1)],
                               in_=xT[NP * c:NP * (c + 1), Q0:Q0 + TL]).then_inc(s_ld, 16)
                c_ld.add(16)
        ld_own = c_ld.v

        @block.gpsimd
        def _(g):
            for c in range(8):
                g.dma_start(out=wcd_b[:, LAT * c:LAT * (c + 1)],
                            in_=wcd[NP * c:NP * (c + 1), :]).then_inc(s_gw, 16)
                c_gw.add(16)
            for c in range(4):
                g.dma_start(out=wqu_b[:, D * c:D * (c + 1)],
                            in_=wqu[NP * c:NP * (c + 1), :]).then_inc(s_gw, 16)
                c_gw.add(16)
            for c in range(4):
                g.dma_start(out=wkv_b[:, 2 * DH * c:2 * DH * (c + 1)],
                            in_=wkv[NP * c:NP * (c + 1), :]).then_inc(s_gw, 16)
                c_gw.add(16)
            g.iota(xsq[:, 0:NP].bitcast(I32), pattern=[[1, NP]], base=0, channel_multiplier=-1)
            g.iota(ksq[0:DH, 0:DH].bitcast(I32), pattern=[[-1, DH]], base=0,
                   channel_multiplier=1).then_inc(s_gp, 1)
            c_gp.add(1)
        gw_all0 = c_gw.v

        @block.vector
        def _(ve):
            def ss(i):
                i.then_inc(s_ve, 1)
                ve.wait_ge(s_ve, c_ve.add(1))
            ss(ve.memset(ones_p1[:], 1.0))
            ss(ve.memset(ones_1p[:], 1.0))
            ss(ve.memset(sc125[:], 0.125))
            ss(ve.memset(epst[:], EPS))
            ss(ve.memset(kmax2[:], 0.0))
            ve.wait_ge(s_gp, c_gp.v)
            ss(ve.tensor_single_scalar(mtri[:], xsq[:, 0:NP].bitcast(I32), 0, ALU.is_ge))
            ss(ve.tensor_single_scalar(id64[:], ksq[0:DH, 0:DH].bitcast(I32), 0, ALU.is_equal))
        ve_const = c_ve.v

        p1_ctx = ExitStack()
        xpart = p1_ctx.enter_context(
            nc.sbuf_tensor("xpart", [NP, 2 * 8 * 512], F32)) if h == 1 else None
        ld_at_slice = {}
        ve_ln_at_slice = {}

        c_l = {0: Ctr(), 1: Ctr()}

        def emit_partner_load(s):
            slot = s % 2
            sem = s_l0 if slot == 0 else s_l1
            war = ve_ln_at_slice.get(s - 2)

            @block.sync
            def _(sync):
                if war is not None:
                    sync.wait_ge(s_ve, war)
                for c in range(8):
                    sync.dma_start(out=xpart[:, (slot * 8 + c) * 512:(slot * 8 + c + 1) * 512],
                                   in_=xT[NP * c:NP * (c + 1), 512 * s:512 * (s + 1)]).then_inc(sem, 16)
                    c_l[slot].add(16)
            ld_at_slice[s] = (sem, c_l[slot].v)

        n_par = (SMAX - TL) // 512
        if n_par > 0:
            emit_partner_load(0)
            if n_par > 1:
                emit_partner_load(1)

        for s in range(NSL):
            own = (512 * s) >= Q0
            slot = s % 2

            def src(c, _s=s, _own=own, _slot=slot):
                if _own:
                    off = 512 * _s - Q0
                    return xown[:, TL * c + off:TL * c + off + 512]
                return xpart[:, (_slot * 8 + c) * 512:(_slot * 8 + c + 1) * 512]

            ld_sem, ld_need = (s_ld, ld_own) if own else ld_at_slice[s]

            with nc.psum_tensor("psA_%d" % s, [1, 512], F32) as psA, \
                 nc.psum_tensor("psB_%d" % s, [1, 512], F32) as psB, \
                 nc.psum_tensor("psRA_%d" % s, [NP, 512], F32) as psRA, \
                 nc.psum_tensor("psRB_%d" % s, [NP, 512], F32) as psRB, \
                 nc.psum_tensor("psL_%d" % s, [NP, 512], F32) as psL, \
                 nc.psum_tensor("psKV_%d" % s, [NP, 512], F32) as psKV, \
                 nc.psum_tensor("psKN_%d" % s, [1, 512], F32) as psKN, \
                 nc.psum_tensor("psVT_%d" % s, [NP, DH], BF16) as psVT:

                se_sq0 = c_se.v
                pe_b0 = c_pe.v

                @block.scalar
                def _(se):
                    se.wait_ge(ld_sem, ld_need)
                    se.wait_ge(s_ve, ve_const)
                    if s > 0:
                        se.wait_ge(s_ve, ve_ln_at_slice[s - 1])
                    for c in range(8):
                        if c >= 2:
                            se.wait_ge(s_pe, pe_b0 + (c - 2) + 1)
                        se.activation(xsq[:, (c % 2) * 512:(c % 2) * 512 + 512], src(c),
                                      AF.Square).then_inc(s_se, 1)
                        c_se.add(1)

                @block.tensor
                def _(te):
                    te.wait_ge(s_ve, ve_const)
                    te.wait_ge(ld_sem, ld_need)
                    for c in range(8):
                        te.matmul(psA[:], ones_p1[:], src(c), start=(c == 0), stop=(c == 7))
                    for c in range(8):
                        te.wait_ge(s_se, se_sq0 + c + 1)
                        te.matmul(psB[:], ones_p1[:], xsq[:, (c % 2) * 512:(c % 2) * 512 + 512],
                                  start=(c == 0), stop=(c == 7)).then_inc(s_pe, 1)
                        c_pe.add(1)
                pe_stats = c_pe.v

                if debug and s == 0:
                    @block.vector
                    def _(ve):
                        ve.wait_ge(s_pe, pe_stats)
                        ve.tensor_copy(qsq[0:1, 0:512], psA[:])

                    @block.sync
                    def _(sync):
                        for c in range(8):
                            sync.dma_start(out=dbg_xp[:, c * 512:(c + 1) * 512],
                                           in_=src(c)).then_inc(s_st, 16)
                            c_st.add(16)

                m_ = stat[:, 0:512]; tb_ = stat[:, 512:1024]
                r_ = stat[:, 1024:1536]; bv_ = stat[:, 1536:2048]

                @block.vector
                def _(ve):
                    def ss(i):
                        i.then_inc(s_ve, 1)
                        ve.wait_ge(s_ve, c_ve.add(1))
                    ve.wait_ge(s_pe, pe_stats)
                    ss(ve.tensor_scalar(m_, psA[:], 1.0 / D, None, ALU.mult))
                    ss(ve.tensor_scalar(tb_, psB[:], 1.0 / D, None, ALU.mult))
                    ss(ve.tensor_tensor(r_, m_, m_, ALU.mult))
                    ss(ve.tensor_tensor(tb_, tb_, r_, ALU.subtract))
                ve_var = c_ve.v

                @block.scalar
                def _(se):
                    se.wait_ge(s_ve, ve_var)
                    se.activation(bv_, tb_, AF.Sqrt, bias=epst[:]).then_inc(s_se, 1)
                    c_se.add(1)
                se_sd = c_se.v

                @block.vector
                def _(ve):
                    def ss(i):
                        i.then_inc(s_ve, 1)
                        ve.wait_ge(s_ve, c_ve.add(1))
                    ve.wait_ge(s_se, se_sd)
                    ss(ve.reciprocal(r_, bv_))
                    ss(ve.tensor_tensor(tb_, m_, r_, ALU.mult))
                    ss(ve.tensor_scalar(bv_, tb_, -1.0, None, ALU.mult))
                ve_rb = c_ve.v

                @block.tensor
                def _(te):
                    te.wait_ge(s_ve, ve_rb)
                    te.matmul(psRA[:], ones_1p[:], r_, start=True, stop=True)
                    te.matmul(psRB[:], ones_1p[:], bv_, start=True, stop=True).then_inc(s_pe, 1)
                    c_pe.add(1)
                pe_bc = c_pe.v

                @block.vector
                def _(ve):
                    ve.wait_ge(s_pe, pe_bc)
                    for c in range(8):
                        ve.tensor_tensor(xsq[:, (c % 2) * 512:(c % 2) * 512 + 512], src(c),
                                         psRA[:], ALU.mult).then_inc(s_ve, 1)
                        ve.wait_ge(s_ve, c_ve.add(1))
                        ve.tensor_tensor(xln[:, c * 512:(c + 1) * 512],
                                         xsq[:, (c % 2) * 512:(c % 2) * 512 + 512], psRB[:],
                                         ALU.add).then_inc(s_ve, 1)
                        c_ve.add(1)
                ve_ln_at_slice[s] = c_ve.v
                ve_ln0 = c_ve.v - 8

                if debug and s == 0:
                    _veln = c_ve.v

                    @block.sync
                    def _(sync):
                        sync.wait_ge(s_ve, _veln)
                        sync.dma_start(out=dbg_cs, in_=xln[:]).then_inc(s_st, 16)
                        sync.dma_start(out=dbg_sm.rearrange("p a -> (p a)").unsqueeze(0),
                                       in_=stat[:, 0:1024]).then_inc(s_st, 16)
                        sync.dma_start(out=dbg_ab[:, 0:512], in_=qsq[0:1, 0:512]).then_inc(s_st, 16)
                        c_st.add(48)

                if (s + 2) < n_par:
                    emit_partner_load(s + 2)

                for lc in range(4):
                    @block.tensor
                    def _(te, _lc=lc):
                        if _lc == 0:
                            te.wait_ge(s_gw, gw_all0)
                        for c in range(8):
                            te.wait_ge(s_ve, ve_ln0 + c + 1)
                            i = te.matmul(psL[:],
                                          wcd_b[:, LAT * c + NP * _lc:LAT * c + NP * (_lc + 1)],
                                          xln[:, c * 512:(c + 1) * 512],
                                          start=(c == 0), stop=(c == 7))
                        i.then_inc(s_pe, 1)
                        c_pe.add(1)
                    pe_l = c_pe.v

                    @block.scalar
                    def _(se, _lc=lc, _pe=pe_l):
                        se.wait_ge(s_pe, _pe)
                        se.copy(lat[:, SMAX * _lc + 512 * s:SMAX * _lc + 512 * (s + 1)],
                                psL[:]).then_inc(s_se, 1)
                        c_se.add(1)
                    se_evl = c_se.v

                    @block.tensor
                    def _(te, _se=se_evl):
                        te.wait_ge(s_se, _se)
                se_lat0 = c_se.v - 4

                @block.tensor
                def _(te):
                    te.wait_ge(s_gw, gw_all0)
                    for lc in range(4):
                        te.wait_ge(s_se, se_lat0 + lc + 1)
                        i = te.matmul(psKV[:], wkv_b[:, 2 * DH * lc:2 * DH * (lc + 1)],
                                      lat[:, SMAX * lc + 512 * s:SMAX * lc + 512 * (s + 1)],
                                      start=(lc == 0), stop=(lc == 3))
                    i.then_inc(s_pe, 1)
                    c_pe.add(1)
                pe_kv = c_pe.v

                @block.scalar
                def _(se):
                    se.wait_ge(s_pe, pe_kv)
                    se.copy(k2[0:DH, 512 * s:512 * (s + 1)], psKV[0:DH, :])
                    se.copy(k2[DH:NP, 512 * s:512 * (s + 1)], psKV[0:DH, :])
                    se.copy(kvt[:, :], psKV[DH:NP, :])
                    se.activation(ksq[:, :], psKV[0:DH, :], AF.Square).then_inc(s_se, 1)
                    c_se.add(1)
                se_kev = c_se.v

                @block.tensor
                def _(te):
                    te.wait_ge(s_se, se_kev)
                    te.matmul(psKN[:], ones_p1[0:DH, :], ksq[:, :], start=True,
                              stop=True).then_inc(s_pe, 1)
                    c_pe.add(1)
                pe_kn = c_pe.v

                @block.vector
                def _(ve):
                    def ss(i):
                        i.then_inc(s_ve, 1)
                        ve.wait_ge(s_ve, c_ve.add(1))
                    ve.wait_ge(s_pe, pe_kn)
                    ss(ve.reduce_max(kmax2[:, 1:2], psKN[:], axis=AX.X))
                    ss(ve.tensor_tensor(kmax2[:, 0:1], kmax2[:, 0:1], kmax2[:, 1:2], ALU.max))

                for j in range(4):
                    @block.tensor
                    def _(te, _j=j):
                        if _j == 0:
                            te.wait_ge(s_se, se_kev)
                        te.transpose(psVT[:, :], kvt[:, NP * _j:NP * (_j + 1)],
                                     id64[:, :]).then_inc(s_pe, 1)
                        c_pe.add(1)
                    pe_vt = c_pe.v

                    @block.vector
                    def _(ve, _j=j, _pe=pe_vt):
                        sc = 4 * s + _j
                        ve.wait_ge(s_pe, _pe)
                        ve.tensor_copy(vo[:, 65 * sc:65 * sc + DH], psVT[:, :])
                        ve.memset(vo[:, 65 * sc + DH:65 * sc + 65], 1.0).then_inc(s_ve, 1)
                        c_ve.add(1)
                    ve_vt = c_ve.v

                    @block.tensor
                    def _(te, _ve=ve_vt):
                        te.wait_ge(s_ve, _ve)

        ve_p1 = c_ve.v
        se_p1 = c_se.v

        p1_ctx.close()
        wcp_b = E(nc.sbuf_tensor("wcp_b", [NP, 8 * D], BF16))

        c_gw2 = Ctr()

        @block.gpsimd
        def _(g):
            g.wait_ge(s_ve, ve_p1)
            g.wait_ge(s_se, se_p1)
            for c in range(8):
                g.dma_start(out=wcp_b[:, D * c:D * (c + 1)],
                            in_=wcp[NP * c:NP * (c + 1), :]).then_inc(s_gw2, 16)
                c_gw2.add(16)
        gw_all = c_gw2.v

        r_slab = ctx.enter_context(nc.sbuf_tensor("r_slab", [1, NH * 512], BF16))
        selA = ctx.enter_context(nc.sbuf_tensor("selA", [1, NP], BF16))
        selB = ctx.enter_context(nc.sbuf_tensor("selB", [1, NP], BF16))

        @block.vector
        def _(ve):
            def ss(i):
                i.then_inc(s_ve, 1)
                ve.wait_ge(s_ve, c_ve.add(1))
            ss(ve.memset(selA[:], 0.0))
            ss(ve.memset(selA[:, 0:DH], 1.0))
            ss(ve.memset(selB[:], 0.0))
            ss(ve.memset(selB[:, DH:NP], 1.0))
        ve_sel = c_ve.v

        ve_prev_macro = 0
        se_prev_group_end = 0
        for m in range(NMAC):
            q0m = Q0 + 512 * m
            nsc = (q0m + 512) // NP
            diag0 = q0m // NP

            with nc.psum_tensor("psQ_%d" % m, [NP, 2 * 512], F32) as psQ, \
                 nc.psum_tensor("psN_%d" % m, [1, 512], F32) as psN, \
                 nc.psum_tensor("psBC_%d" % m, [NP, 1], F32) as psBC:

                se_q_at = {}
                sq_q = []
                for ec in range(8):
                    par = ec % 2

                    @block.tensor
                    def _(te, _ec=ec, _par=par, _vpm=ve_prev_macro):
                        if _ec == 0 and m == 0:
                            te.wait_ge(s_ve, ve_p1)
                            te.wait_ge(s_se, se_p1)
                        if _ec == 0 and _vpm:
                            te.wait_ge(s_ve, _vpm)
                        if _ec >= 2:
                            te.wait_ge(s_se, se_q_at[_ec - 2])
                        for lc in range(4):
                            i = te.matmul(psQ[:, 512 * _par:512 * (_par + 1)],
                                          wqu_b[:, D * lc + NP * _ec:D * lc + NP * (_ec + 1)],
                                          lat[:, SMAX * lc + q0m:SMAX * lc + q0m + 512],
                                          start=(lc == 0), stop=(lc == 3))
                        i.then_inc(s_pe, 1)
                        c_pe.add(1)
                    pe_q = c_pe.v

                    def fq(se, _ec=ec, _par=par, _pe=pe_q):
                        se.wait_ge(s_pe, _pe)
                        se.copy(qm[:, 512 * _ec:512 * (_ec + 1)], psQ[:, 512 * _par:512 * (_par + 1)])
                        se.activation(qsq[:, :], psQ[:, 512 * _par:512 * (_par + 1)],
                                      AF.Square).then_inc(s_se, 1)
                    sq_q.append(fq)
                    c_se.add(1)
                    se_q_at[ec] = c_se.v

                    @block.tensor
                    def _(te, _ec=ec, _se=se_q_at[ec]):
                        te.wait_ge(s_se, _se)
                        te.matmul(psN[:], ones_p1[:], qsq[:, :], start=(_ec == 0),
                                  stop=(_ec == 7)).then_inc(s_pe, 1)
                        c_pe.add(1)
                pe_n = c_pe.v

                @block.scalar
                def _(se, _sq=tuple(sq_q)):
                    for f in _sq:
                        f(se)

                @block.vector
                def _(ve):
                    def ss(i):
                        i.then_inc(s_ve, 1)
                        ve.wait_ge(s_ve, c_ve.add(1))
                    ve.wait_ge(s_pe, pe_n)
                    ss(ve.reduce_max(qmax2[:, 0:1], psN[:], axis=AX.X))
                ve_q = c_ve.v

                @block.scalar
                def _(se):
                    se.wait_ge(s_ve, ve_q)
                    se.activation(sb11[:, 0:1], qmax2[:, 0:1], AF.Sqrt)
                    se.activation(sb11[:, 1:2], kmax2[:, 0:1], AF.Sqrt).then_inc(s_se, 1)
                    c_se.add(1)
                se_b = c_se.v

                @block.vector
                def _(ve):
                    def ss(i):
                        i.then_inc(s_ve, 1)
                        ve.wait_ge(s_ve, c_ve.add(1))
                    ve.wait_ge(s_se, se_b)
                    ss(ve.tensor_tensor(sb11[:, 2:3], sb11[:, 0:1], sb11[:, 1:2], ALU.mult))
                    ss(ve.tensor_scalar(sb11[:, 3:4], sb11[:, 2:3], -0.125, None, ALU.mult))
                ve_b = c_ve.v

                @block.tensor
                def _(te):
                    te.wait_ge(s_ve, ve_b)
                    te.matmul(psBC[:], ones_1p[:], sb11[:, 3:4], start=True,
                              stop=True).then_inc(s_pe, 1)
                    c_pe.add(1)
                pe_bb = c_pe.v

                @block.vector
                def _(ve):
                    ve.wait_ge(s_pe, pe_bb)
                    ve.tensor_copy(biasm[:], psBC[:]).then_inc(s_ve, 1)
                    c_ve.add(1)
                ve_bias = c_ve.v

            with nc.psum_tensor("psS_%d_0" % m, [NP, 1024], F32) as psSa, \
                 nc.psum_tensor("psS_%d_1" % m, [NP, 1024], F32) as psSb, \
                 nc.psum_tensor("psS_%d_2" % m, [NP, 1024], F32) as psSc, \
                 nc.psum_tensor("psCA_%d" % m, [65, 512], F32) as psCA, \
                 nc.psum_tensor("psCB_%d" % m, [65, 512], F32) as psCB:
                psS_l = [psSa, psSb, psSc]
                sq_m, vq_m = [], []
                for ec in range(8):
                    tq = []
                    pe_score_at = {}
                    pe_ctx_at = {}
                    exp_base = c_se.v
                    ve_base = c_ve.v
                    exp_at = {sc: exp_base + sc + 1 for sc in range(nsc)}
                    mask_at = {sc: ve_base + (sc - diag0) + 1
                               for sc in range(diag0, nsc)}

                    def attb(sc):
                        return att[:, 1024 * (sc % 4):1024 * (sc % 4) + 1024]

                    def stage_score(sc, _ec=ec, _spge=se_prev_group_end):
                        psS = psS_l[sc % 3]
                        col0 = (NP * sc - q0m) if sc >= diag0 else 0

                        def f(te, _sc=sc, _psS=psS, _c0=col0, _e=_ec, _sw=_spge):
                            if _sc == 0:
                                te.wait_ge(s_ve, ve_bias)
                                if _sw:
                                    te.wait_ge(s_se, _sw)
                            te.matmul(_psS[:, _c0:512],
                                      k2[0:DH, NP * _sc:NP * (_sc + 1)],
                                      qm[0:DH, 512 * _e + _c0:512 * (_e + 1)],
                                      start=True, stop=True, tile_position=(0, 0))
                            te.matmul(_psS[:, 512 + _c0:1024],
                                      k2[DH:NP, NP * _sc:NP * (_sc + 1)],
                                      qm[DH:NP, 512 * _e + _c0:512 * (_e + 1)],
                                      start=True, stop=True,
                                      tile_position=(DH, 0)).then_inc(s_pe, 1)
                        tq.append(f)
                        c_pe.add(1)
                        pe_score_at[sc] = c_pe.v

                    def stage_ctx(sc):
                        col0 = (NP * sc - q0m) if sc >= diag0 else 0
                        gate = (s_ve, mask_at[sc]) if sc >= diag0 else (s_se, exp_at[sc])

                        def f(te, _sc=sc, _c0=col0, _g=gate, _nsc=nsc):
                            te.wait_ge(_g[0], _g[1])
                            ab = attb(_sc)
                            te.matmul(psCA[:, _c0:512], vo[:, 65 * _sc:65 * (_sc + 1)],
                                      ab[:, _c0:512],
                                      start=(_sc == 0), stop=(_sc == _nsc - 1))
                            te.matmul(psCB[:, _c0:512], vo[:, 65 * _sc:65 * (_sc + 1)],
                                      ab[:, 512 + _c0:1024],
                                      start=(_sc == 0),
                                      stop=(_sc == _nsc - 1)).then_inc(s_pe, 1)
                        tq.append(f)
                        c_pe.add(1)
                        pe_ctx_at[sc] = c_pe.v

                    for sc in range(min(3, nsc)):
                        stage_score(sc)
                    for sc in range(nsc):
                        stage_ctx(sc)
                        if sc + 3 < nsc:
                            stage_score(sc + 3)

                    for sc in range(nsc):
                        col0 = (NP * sc - q0m) if sc >= diag0 else 0

                        def f(se, _sc=sc, _pe=pe_score_at[sc], _c0=col0):
                            se.wait_ge(s_pe, _pe)
                            ab = attb(_sc)
                            se.activation(
                                ab.rearrange("p (a w) -> p a w", a=2)[:, :, _c0:512],
                                psS_l[_sc % 3][:].rearrange(
                                    "p (a w) -> p a w", a=2)[:, :, _c0:512],
                                AF.Exp, bias=biasm[:], scale=sc125[:]).then_inc(s_se, 1)
                        sq_m.append(f)
                        c_se.add(1)
                        assert c_se.v == exp_at[sc]
                    pe_all_ctx = pe_ctx_at[nsc - 1]
                    for hh in range(2):
                        psC = psCA if hh == 0 else psCB
                        head = 2 * ec + hh

                        def f(se, _psC=psC, _pe=pe_all_ctx, _head=head):
                            se.wait_ge(s_pe, _pe)
                            se.copy(cs[(_head % 2) * DH:(_head % 2) * DH + DH,
                                       512 * (_head // 2):512 * (_head // 2 + 1)],
                                    _psC[0:DH, :])
                            se.copy(r_slab[:, 512 * _head:512 * (_head + 1)],
                                    _psC[DH:65, :]).then_inc(s_se, 1)
                        sq_m.append(f)
                        c_se.add(1)

                    for sc in range(diag0, nsc):
                        def f(ve, _sc=sc, _se=exp_at[sc]):
                            ve.wait_ge(s_se, _se)
                            ab = attb(_sc)
                            c0 = NP * _sc - q0m
                            a3 = ab.rearrange("p (a w) -> p a w", a=2)[:, :, c0:c0 + NP]
                            m3 = mtri[:].unsqueeze(1).broadcast_to([NP, 2, NP])
                            ve.tensor_tensor(a3, a3, m3, ALU.mult).then_inc(s_ve, 1)
                        vq_m.append(f)
                        c_ve.add(1)
                        assert c_ve.v == mask_at[sc]

                    @block.tensor
                    def _(te, _tq=tuple(tq)):
                        for f in _tq:
                            f(te)
                    se_prev_group_end = c_se.v

                @block.scalar
                def _(se, _sq=tuple(sq_m)):
                    for f in _sq:
                        f(se)

                @block.vector
                def _(ve, _vq=tuple(vq_m)):
                    for f in _vq:
                        f(ve)
                se_evac_all = c_se.v
                ve_evac_all = c_ve.v

            with nc.psum_tensor("psNB_%d" % m, [NP, 2 * 512], F32) as psNB, \
                 nc.psum_tensor("psP_%d" % m, [NP, 2 * 512], F32) as psP:
                ve_norm_at = {}
                for c in range(8):
                    parn = c % 2

                    @block.tensor
                    def _(te, _c=c, _parn=parn):
                        if _c == 0:
                            te.wait_ge(s_ve, ve_evac_all)
                            te.wait_ge(s_se, se_evac_all)
                        if _c >= 2:
                            te.wait_ge(s_ve, ve_norm_at[_c - 2])
                        te.matmul(psNB[:, 512 * _parn:512 * (_parn + 1)], selA[:],
                                  r_slab[:, 512 * (2 * _c):512 * (2 * _c + 1)],
                                  start=True, stop=False)
                        te.matmul(psNB[:, 512 * _parn:512 * (_parn + 1)], selB[:],
                                  r_slab[:, 512 * (2 * _c + 1):512 * (2 * _c + 2)],
                                  start=False, stop=True).then_inc(s_pe, 1)
                        c_pe.add(1)
                    pe_nb = c_pe.v

                    @block.vector
                    def _(ve, _c=c, _parn=parn, _pe=pe_nb):
                        ve.wait_ge(s_pe, _pe)
                        if _c == 0:
                            ve.wait_ge(s_ve, ve_sel)
                        nb = psNB[:, 512 * _parn:512 * (_parn + 1)]
                        ve.reciprocal(nb, nb).then_inc(s_ve, 1)
                        ve.wait_ge(s_ve, c_ve.add(1))
                        dst = cs[:, 512 * _c:512 * (_c + 1)]
                        ve.tensor_tensor(dst, dst, nb, ALU.mult).then_inc(s_ve, 1)
                        c_ve.add(1)
                    ve_norm_at[c] = c_ve.v
                ve_cs = c_ve.v

                pe_proj_at = {}
                for dm in range(8):
                    parp = dm % 2

                    @block.tensor
                    def _(te, _dm=dm, _parp=parp):
                        if _dm == 0:
                            te.wait_ge(s_ve, ve_cs)
                            te.wait_ge(s_gw2, gw_all)
                        war = pe_proj_at.get(_dm - 2)
                        if war is not None:
                            te.wait_ge(s_ve, war)
                        for c in range(8):
                            i = te.matmul(psP[:, 512 * _parp:512 * (_parp + 1)],
                                          wcp_b[:, D * c + NP * _dm:D * c + NP * (_dm + 1)],
                                          cs[:, 512 * c:512 * (c + 1)],
                                          start=(c == 0), stop=(c == 7))
                        i.then_inc(s_pe, 1)
                        c_pe.add(1)
                    pe_p = c_pe.v

                    @block.vector
                    def _(ve, _dm=dm, _parp=parp, _pe=pe_p, _m=m):
                        ve.wait_ge(s_pe, _pe)
                        dst = xown[:, TL * _dm + 512 * _m:TL * _dm + 512 * (_m + 1)]
                        ve.tensor_tensor(dst, dst, psP[:, 512 * _parp:512 * (_parp + 1)],
                                         ALU.add).then_inc(s_ve, 1)
                        c_ve.add(1)
                    pe_proj_at[dm] = c_ve.v
            ve_prev_macro = c_ve.v

        ve_fin = c_ve.v

        @block.sync
        def _(sync):
            sync.wait_ge(s_ve, ve_fin)
            for c in range(8):
                sync.dma_start(out=x1[NP * c:NP * (c + 1), :],
                               in_=xown[:, TL * c:TL * (c + 1)]).then_inc(s_st, 16)
                c_st.add(16)
            if debug:
                sync.dma_start(out=dbg_lat, in_=lat[:]).then_inc(s_st, 16)
                sync.dma_start(out=dbg_k2, in_=k2[:]).then_inc(s_st, 16)
                sync.dma_start(out=dbg_vo, in_=vo[:]).then_inc(s_st, 16)
                sync.dma_start(out=dbg_qm, in_=qm[:]).then_inc(s_st, 16)
                c_st.add(64)
            sync.wait_ge(s_st, c_st.v)

    return nc



class LNEnv:

    def __init__(self, nc, ctx, block, sems, ctrs, name="ln", maxw=512):
        E = ctx.enter_context
        self.maxw = maxw
        self.nc = nc
        self.block = block
        self.xsq = E(nc.sbuf_tensor(name + "_xsq", [NP, 2 * maxw], F32))
        self.stat = E(nc.sbuf_tensor(name + "_stat", [1, 4 * maxw], F32))
        self.ones_p1 = E(nc.sbuf_tensor(name + "_o1", [NP, 1], F32))
        self.ones_1p = E(nc.sbuf_tensor(name + "_o2", [1, NP], F32))
        self.epst = E(nc.sbuf_tensor(name + "_eps", [1, 1], F32))
        (self.s_pe, self.s_se, self.s_ve) = sems
        (self.c_pe, self.c_se, self.c_ve) = ctrs
        self.ve_const = None

    def emit_consts(self):
        s_ve, c_ve = self.s_ve, self.c_ve

        @self.block.vector
        def _(ve):
            def ss(i):
                i.then_inc(s_ve, 1)
                ve.wait_ge(s_ve, c_ve.add(1))
            ss(ve.memset(self.ones_p1[:], 1.0))
            ss(ve.memset(self.ones_1p[:], 1.0))
            ss(ve.memset(self.epst[:], EPS))
        self.ve_const = c_ve.v

    _uid = [0]

    def emit_slice(self, src, dst_store, pre_waits=(), scalar_pre_waits=(), n_chunks=8, width=512):
        nc, block = self.nc, self.block
        s_pe, s_se, s_ve = self.s_pe, self.s_se, self.s_ve
        c_pe, c_se, c_ve = self.c_pe, self.c_se, self.c_ve
        xsq, stat = self.xsq, self.stat

        LNEnv._uid[0] += 1
        u = LNEnv._uid[0]
        with nc.psum_tensor("lnA%d" % u, [1, width], F32) as psA, \
             nc.psum_tensor("lnB%d" % u, [1, width], F32) as psB, \
             nc.psum_tensor("lnRA%d" % u, [NP, width], F32) as psRA, \
             nc.psum_tensor("lnRB%d" % u, [NP, width], F32) as psRB:

            se_sq0 = c_se.v
            pe_b0 = c_pe.v

            @block.scalar
            def _(se):
                for sem, cnt in pre_waits:
                    se.wait_ge(sem, cnt)
                for sem, cnt in scalar_pre_waits:
                    se.wait_ge(sem, cnt)
                se.wait_ge(s_ve, self.ve_const)
                for c in range(n_chunks):
                    if c >= 2:
                        se.wait_ge(s_pe, pe_b0 + (c - 2) + 1)
                    se.activation(xsq[:, (c % 2) * self.maxw:(c % 2) * self.maxw + width],
                                  src(c), AF.Square).then_inc(s_se, 1)
                    c_se.add(1)

            nq = (width + 511) // 512

            @block.tensor
            def _(te):
                for sem, cnt in pre_waits:
                    te.wait_ge(sem, cnt)
                te.wait_ge(s_ve, self.ve_const)
                for c in range(n_chunks):
                    for q in range(nq):
                        te.matmul(psA[:, 512 * q:512 * (q + 1)], self.ones_p1[:],
                                  src(c)[:, 512 * q:512 * (q + 1)], start=(c == 0),
                                  stop=(c == n_chunks - 1))
                for c in range(n_chunks):
                    te.wait_ge(s_se, se_sq0 + c + 1)
                    for q in range(nq):
                        i = te.matmul(psB[:, 512 * q:512 * (q + 1)], self.ones_p1[:],
                                      xsq[:, (c % 2) * self.maxw + 512 * q:
                                          (c % 2) * self.maxw + 512 * (q + 1)],
                                      start=(c == 0), stop=(c == n_chunks - 1))
                    i.then_inc(s_pe, 1)
                    c_pe.add(1)
            pe_stats = c_pe.v

            mw = self.maxw
            m_ = stat[:, 0:width]; tb_ = stat[:, mw:mw + width]
            r_ = stat[:, 2 * mw:2 * mw + width]; bv_ = stat[:, 3 * mw:3 * mw + width]

            @block.vector
            def _(ve):
                def ss(i):
                    i.then_inc(s_ve, 1)
                    ve.wait_ge(s_ve, c_ve.add(1))
                ve.wait_ge(s_pe, pe_stats)
                ss(ve.tensor_scalar(m_, psA[:], 1.0 / D, None, ALU.mult))
                ss(ve.tensor_scalar(tb_, psB[:], 1.0 / D, None, ALU.mult))
                ss(ve.tensor_tensor(r_, m_, m_, ALU.mult))
                ss(ve.tensor_tensor(tb_, tb_, r_, ALU.subtract))
            ve_var = c_ve.v

            @block.scalar
            def _(se):
                se.wait_ge(s_ve, ve_var)
                se.activation(bv_, tb_, AF.Sqrt, bias=self.epst[:]).then_inc(s_se, 1)
                c_se.add(1)
            se_sd = c_se.v

            @block.vector
            def _(ve):
                def ss(i):
                    i.then_inc(s_ve, 1)
                    ve.wait_ge(s_ve, c_ve.add(1))
                ve.wait_ge(s_se, se_sd)
                ss(ve.reciprocal(r_, bv_))
                ss(ve.tensor_tensor(tb_, m_, r_, ALU.mult))
                ss(ve.tensor_scalar(bv_, tb_, -1.0, None, ALU.mult))
            ve_rb = c_ve.v

            @block.tensor
            def _(te):
                te.wait_ge(s_ve, ve_rb)
                for q in range(nq):
                    te.matmul(psRA[:, 512 * q:512 * (q + 1)], self.ones_1p[:],
                              r_[:, 512 * q:512 * (q + 1)], start=True, stop=True)
                    i = te.matmul(psRB[:, 512 * q:512 * (q + 1)], self.ones_1p[:],
                                  bv_[:, 512 * q:512 * (q + 1)], start=True, stop=True)
                i.then_inc(s_pe, 1)
                c_pe.add(1)
            pe_bc = c_pe.v

            dst_store(psRA, psRB, pe_bc)



def build_l0_gate(debug=False):
    nc = bass.Bass("TRN2", target_bir_lowering=False)
    x1a = nc.dram_tensor("x1a", [D, TL], F32, kind="ExternalInput").ap()
    x1b = nc.dram_tensor("x1b", [D, TL], F32, kind="ExternalInput").ap()
    wg = nc.dram_tensor("wg", [D, 8], F32, kind="ExternalInput").ap()
    bg = nc.dram_tensor("bg", [1, 8], F32, kind="ExternalInput").ap()
    gout = nc.dram_tensor("gout", [1, 8], F32, kind="ExternalOutput").ap()
    if debug:
        dbg_sm = nc.dram_tensor("dbg_sm", [1, 64], F32, kind="ExternalOutput").ap()
        dbg_acc = nc.dram_tensor("dbg_acc", [NP, 8], F32, kind="ExternalOutput").ap()

    with ExitStack() as ctx:
        E = ctx.enter_context
        xbuf = E(nc.sbuf_tensor("xbuf", [NP, 2 * 8 * 512], F32))
        acc = E(nc.sbuf_tensor("acc", [NP, 8], F32))
        rs_t = E(nc.sbuf_tensor("rs_t", [NP, 1], F32))
        wgs = E(nc.sbuf_tensor("wgs", [NP, 8 * 8], F32))
        bgs = E(nc.sbuf_tensor("bgs", [1, 8], F32))
        tsc = E(nc.sbuf_tensor("tsc", [NP, 512], F32))
        sm = E(nc.sbuf_tensor("sm", [1, 64], F32))
        iot8 = E(nc.sbuf_tensor("iot8", [1, 8], I32))
        iot8f = E(nc.sbuf_tensor("iot8f", [1, 8], F32))
        s_ld = E(nc.semaphore("s_ld"))
        s_l1 = E(nc.semaphore("s_l1"))
        s_pe = E(nc.semaphore("s_pe"))
        s_se = E(nc.semaphore("s_se"))
        s_ve = E(nc.semaphore("s_ve"))
        s_gp = E(nc.semaphore("s_gp"))
        s_st = E(nc.semaphore("s_st"))
        block = E(nc.Block())
        c_ld, c_l1, c_pe, c_se, c_ve, c_gp, c_st = (Ctr() for _ in range(7))
        ln = LNEnv(nc, ctx, block, (s_pe, s_se, s_ve), (c_pe, c_se, c_ve), "g")
        ln.emit_consts()

        @block.vector
        def _(ve):
            ve.memset(acc[:], 0.0).then_inc(s_ve, 1)
            ve.wait_ge(s_ve, c_ve.add(1))
            ve.memset(sm[:], 0.0).then_inc(s_ve, 1)
            ve.wait_ge(s_ve, c_ve.add(1))

        @block.gpsimd
        def _(g):
            g.iota(iot8[:], pattern=[[1, 8]], base=0, channel_multiplier=0).then_inc(s_gp, 1)
            c_gp.add(1)

        s_lg = E(nc.semaphore("s_lg"))
        c_lg = Ctr()

        @block.sync
        def _(sync):
            sync.dma_start(out=wgs[:].rearrange("p (a b) -> p a b", b=8),
                           in_=wg.rearrange("(a p) b -> p a b", p=NP)).then_inc(s_lg, 16)
            sync.dma_start(out=bgs[:], in_=bg).then_inc(s_lg, 16)
            c_lg.add(32)
        ld_w = c_lg.v

        slot_sem = [s_ld, s_l1]
        slot_ctr = [c_ld, c_l1]
        ve_ln_done = {}
        for si in range(8):
            half, s = si // 4, si % 4
            xin = x1a if half == 0 else x1b
            slot = si % 2
            sem, ctr2 = slot_sem[slot], slot_ctr[slot]
            war = ve_ln_done.get(si - 2)

            @block.sync
            def _(sync, _s=s, _slot=slot, _xin=xin, _war=war, _sem=sem):
                if _war is not None:
                    sync.wait_ge(s_ve, _war)
                for c in range(8):
                    sync.dma_start(out=xbuf[:, (_slot * 8 + c) * 512:(_slot * 8 + c + 1) * 512],
                                   in_=_xin[NP * c:NP * (c + 1), 512 * _s:512 * (_s + 1)]).then_inc(_sem, 16)
                    ctr2.add(16)
            ld_cnt = ctr2.v

            def src(c, _slot=slot):
                return xbuf[:, (_slot * 8 + c) * 512:(_slot * 8 + c + 1) * 512]

            def store(psRA, psRB, pe_bc, _slot=slot):
                @block.vector
                def _(ve):
                    def ss(i):
                        i.then_inc(s_ve, 1)
                        ve.wait_ge(s_ve, c_ve.add(1))
                    ve.wait_ge(s_pe, pe_bc)
                    for c in range(8):
                        ss(ve.tensor_tensor(tsc[:], src(c, _slot), psRA[:], ALU.mult))
                        ss(ve.reduce_sum(rs_t[:], tsc[:], axis=AX.X))
                        ss(ve.tensor_tensor(acc[:, c:c + 1], acc[:, c:c + 1], rs_t[:], ALU.add))
                    ss(ve.reduce_sum(rs_t[0:1, :], psRB[0:1, :], axis=AX.X))
                    ss(ve.tensor_tensor(sm[:, 0:1], sm[:, 0:1], rs_t[0:1, :], ALU.add))
                ve_ln_done[si] = c_ve.v

            ln.emit_slice(src, store, pre_waits=[(sem, ld_cnt)])

        ve_acc = c_ve.v

        with nc.psum_tensor("psG", [1, 8], F32) as psG, \
             nc.psum_tensor("psW", [1, 8], F32) as psW:
            @block.vector
            def _(ve):
                ve.wait_ge(s_ve, ve_acc)
                ve.tensor_scalar(acc[:], acc[:], 1.0 / T, None, ALU.mult).then_inc(s_ve, 1)
                c_ve.add(1)
            ve_sc = c_ve.v

            @block.tensor
            def _(te):
                te.wait_ge(s_ve, ve_sc)
                te.wait_ge(s_lg, ld_w)
                for c in range(8):
                    te.matmul(psG[:], acc[:, c:c + 1], wgs[:, 8 * c:8 * (c + 1)],
                              start=(c == 0), stop=(c == 7))
                for c in range(8):
                    i = te.matmul(psW[:], ln.ones_p1[:], wgs[:, 8 * c:8 * (c + 1)],
                                  start=(c == 0), stop=(c == 7))
                i.then_inc(s_pe, 1)
                c_pe.add(1)
            pe_g = c_pe.v

            lg = sm[:, 8:16]; ex = sm[:, 16:24]; t8 = sm[:, 24:32]; t8b = sm[:, 32:40]

            @block.vector
            def _(ve):
                def ss(i):
                    i.then_inc(s_ve, 1)
                    ve.wait_ge(s_ve, c_ve.add(1))
                ve.wait_ge(s_pe, pe_g)
                ve.wait_ge(s_gp, c_gp.v)
                ve.wait_ge(s_lg, ld_w)
                ss(ve.tensor_scalar(sm[:, 1:2], sm[:, 0:1], 1.0 / T, None, ALU.mult))
                ss(ve.tensor_scalar(lg, psW[0:1, :], sm[:, 1:2], None, ALU.mult))
                ss(ve.tensor_tensor(lg, lg, psG[0:1, :], ALU.add))
                ss(ve.tensor_tensor(lg, lg, bgs[:], ALU.add))
                ss(ve.reduce_max(sm[:, 2:3], lg, axis=AX.X))
                ss(ve.tensor_scalar(ex, lg, sm[:, 2:3], None, ALU.subtract))
            ve_l = c_ve.v

            @block.scalar
            def _(se):
                se.wait_ge(s_ve, ve_l)
                se.activation(ex, ex, AF.Exp).then_inc(s_se, 1)
                c_se.add(1)
            se_e = c_se.v

            @block.vector
            def _(ve):
                def ss(i):
                    i.then_inc(s_ve, 1)
                    ve.wait_ge(s_ve, c_ve.add(1))
                ve.wait_ge(s_se, se_e)
                for j in range(8):
                    ss(ve.memset(iot8f[:, j:j + 1], float(j)))
                ss(ve.reduce_max(sm[:, 3:4], ex, axis=AX.X))
                ss(ve.tensor_scalar(t8, ex, sm[:, 3:4], None, ALU.is_ge))
                ss(ve.tensor_tensor(t8b, t8, iot8f[:], ALU.mult))
                ss(ve.reduce_max(sm[:, 4:5], t8b, axis=AX.X))
                ss(ve.tensor_scalar(t8, t8, -1.0, 1.0, ALU.mult, ALU.add))
                ss(ve.tensor_tensor(t8, ex, t8, ALU.mult))
                ss(ve.reduce_max(sm[:, 5:6], t8, axis=AX.X))
                ss(ve.tensor_scalar(t8b, t8, sm[:, 5:6], None, ALU.is_ge))
                ss(ve.tensor_tensor(t8b, t8b, iot8f[:], ALU.mult))
                ss(ve.reduce_max(sm[:, 6:7], t8b, axis=AX.X))
                ss(ve.tensor_tensor(sm[:, 7:8], sm[:, 3:4], sm[:, 5:6], ALU.add))
                ss(ve.reciprocal(sm[:, 40:41], sm[:, 7:8]))
                ss(ve.tensor_tensor(sm[:, 41:42], sm[:, 3:4], sm[:, 40:41], ALU.mult))
                ss(ve.tensor_tensor(sm[:, 42:43], sm[:, 5:6], sm[:, 40:41], ALU.mult))
                ss(ve.tensor_copy(sm[:, 48:49], sm[:, 4:5]))
                ss(ve.tensor_copy(sm[:, 49:50], sm[:, 6:7]))
                ss(ve.tensor_copy(sm[:, 50:51], sm[:, 41:42]))
                ss(ve.tensor_copy(sm[:, 51:52], sm[:, 42:43]))
                ss(ve.memset(sm[:, 52:56], 0.0))
            ve_fin = c_ve.v

        @block.sync
        def _(sync):
            sync.wait_ge(s_ve, ve_fin)
            sync.dma_start(out=gout, in_=sm[:, 48:56]).then_inc(s_st, 16)
            c_st.add(16)
            if debug:
                sync.dma_start(out=dbg_sm, in_=sm[:]).then_inc(s_st, 16)
                sync.dma_start(out=dbg_acc, in_=acc[:]).then_inc(s_st, 16)
                c_st.add(32)
            sync.wait_ge(s_st, c_st.v)
    return nc



def build_l0_ffn(n_groups=12, stage=5):
    nc = bass.Bass("TRN2", target_bir_lowering=False)
    x1 = nc.dram_tensor("x1", [D, TL], F32, kind="ExternalInput").ap()
    gates = nc.dram_tensor("gates", [1, 8], F32, kind="ExternalInput").ap()
    alpha = nc.dram_tensor("alpha", [1, 1], F32, kind="ExternalInput").ap()
    w1m = nc.dram_tensor("w1m", [D, 4 * D], F32, kind="ExternalInput").ap()
    b1m = nc.dram_tensor("b1m", [1, 4 * D], F32, kind="ExternalInput").ap()
    w2m = nc.dram_tensor("w2m", [4 * D, D], F32, kind="ExternalInput").ap()
    b2m = nc.dram_tensor("b2m", [1, D], F32, kind="ExternalInput").ap()
    w1g = nc.dram_tensor("w1g", [2, D, 4 * D], F32, kind="ExternalInput").ap()
    b1g = nc.dram_tensor("b1g", [2, 4 * D], F32, kind="ExternalInput").ap()
    w2g = nc.dram_tensor("w2g", [2, 4 * D, D], F32, kind="ExternalInput").ap()
    b2g = nc.dram_tensor("b2g", [2, D], F32, kind="ExternalInput").ap()
    x2 = nc.dram_tensor("x2", [D, TL], F32, kind="ExternalOutput").ap()

    with ExitStack() as ctx:
        E = ctx.enter_context
        xr = E(nc.sbuf_tensor("xr", [NP, 8 * TL], F32))
        xln = E(nc.sbuf_tensor("xln", [NP, 8 * TL], BF16))
        hblk = E(nc.sbuf_tensor("hblk", [NP, 8 * TL], BF16))
        w1b = E(nc.sbuf_tensor("w1b", [NP, 2 * 8 * D], BF16))
        w2b = E(nc.sbuf_tensor("w2b", [NP, 2 * 8 * D], BF16))
        bt = E(nc.sbuf_tensor("bt", [NP, 3 * 32 + 3 * 8 + 16], F32))
        btin = E(nc.sbuf_tensor("btin", [32, NP], F32))
        id32 = E(nc.sbuf_tensor("id32", [32, 32], F32))
        id32i = E(nc.sbuf_tensor("id32i", [32, 32], I32))
        ones_1p = E(nc.sbuf_tensor("ones_1pq", [1, NP], F32))
        smg = E(nc.sbuf_tensor("smg", [1, 16], F32))
        s_lx = E(nc.semaphore("s_lx"))
        s_lw = E(nc.semaphore("s_lw"))
        s_lw1 = E(nc.semaphore("s_lw1"))
        s_lv = E(nc.semaphore("s_lv"))
        s_lv1 = E(nc.semaphore("s_lv1"))
        s_lb = E(nc.semaphore("s_lb"))
        s_lb2 = E(nc.semaphore("s_lb2"))
        s_pe = E(nc.semaphore("s_pe"))
        s_se = E(nc.semaphore("s_se"))
        s_ve = E(nc.semaphore("s_ve"))
        s_gp = E(nc.semaphore("s_gp"))
        s_st = E(nc.semaphore("s_st"))
        block = E(nc.Block())
        c_lx, c_lb, c_pe, c_se, c_ve, c_gp, c_st = (Ctr() for _ in range(7))
        c_lw = [Ctr(), Ctr()]
        c_lv = [Ctr(), Ctr()]
        ln = LNEnv(nc, ctx, block, (s_pe, s_se, s_ve), (c_pe, c_se, c_ve), "f")
        ln.emit_consts()

        @block.sync
        def _(sync):
            for c in range(8):
                sync.dma_start(out=xr[:, TL * c:TL * (c + 1)],
                               in_=x1[NP * c:NP * (c + 1), :]).then_inc(s_lx, 16)
                c_lx.add(16)
        ld_x = c_lx.v

        @block.sync
        def _(sync):
            sync.dma_start(out=btin[:], in_=b1m.rearrange("a (r p) -> (a r) p", p=NP)).then_inc(s_lb, 16)
            c_lb.add(16)
        b_jobs = [] if (5 if stage in (40, 41) else stage) < 2 else [(b1g[0:1, :].rearrange("a (r p) -> (a r) p", p=NP), 32, 32),
                  (b1g[1:2, :].rearrange("a (r p) -> (a r) p", p=NP), 32, 64),
                  (b2m.rearrange("a (r p) -> (a r) p", p=NP), 8, 96),
                  (b2g[0:1, :].rearrange("a (r p) -> (a r) p", p=NP), 8, 104),
                  (b2g[1:2, :].rearrange("a (r p) -> (a r) p", p=NP), 8, 112)]

        @block.gpsimd
        def _(g):
            g.iota(id32i[:], pattern=[[-1, 32]], base=0, channel_multiplier=1).then_inc(s_gp, 1)
            c_gp.add(1)
        STG = stage
        _stg_base = 5 if stage in (40, 41) else stage

        @block.vector
        def _(ve):
            def ss(i):
                i.then_inc(s_ve, 1)
                ve.wait_ge(s_ve, c_ve.add(1))
            ss(ve.memset(ones_1p[:], 1.0))
            ve.wait_ge(s_gp, c_gp.v)
            ss(ve.tensor_copy(id32[:], id32i[:]))
            ss(ve.tensor_single_scalar(id32[:], id32[:], 0.0, ALU.is_equal))
        ve_id = c_ve.v

        with nc.psum_tensor("psBT", [NP, 32], F32) as psBT:
          if STG >= 2:
            prev_ld = c_lb.v
            for (ap_in, nrow, dstcol) in [(None, 32, 0)] + b_jobs:
                if ap_in is not None:
                    @block.sync
                    def _(sync, _ap=ap_in, _n=nrow, _ve=None):
                        sync.wait_ge(s_ve, ve_tr_done)
                        sync.dma_start(out=btin[0:_n, :], in_=_ap).then_inc(s_lb, 16)
                        c_lb.add(16)
                ld_b = c_lb.v

                @block.tensor
                def _(te, _n=nrow, _ld=ld_b):
                    te.wait_ge(s_lb, _ld)
                    te.wait_ge(s_ve, ve_id)
                    te.transpose(psBT[:, 0:_n], btin[0:_n, :], id32[0:_n, 0:_n]).then_inc(s_pe, 1)
                    c_pe.add(1)
                pe_tr = c_pe.v

                @block.vector
                def _(ve, _n=nrow, _col=dstcol, _pe=pe_tr):
                    ve.wait_ge(s_pe, _pe)
                    ve.tensor_copy(bt[:, _col:_col + _n], psBT[:, 0:_n]).then_inc(s_ve, 1)
                    c_ve.add(1)
                ve_tr_done = c_ve.v

        if STG < 3:
            ve_g = c_ve.v
            ve_co = c_ve.v
        c_lb2 = Ctr()

        @block.sync
        def _(sync):
            if STG < 3:
                return
            sync.dma_start(out=smg[:, 0:8], in_=gates).then_inc(s_lb2, 16)
            sync.dma_start(out=smg[:, 8:9], in_=alpha).then_inc(s_lb2, 16)
            c_lb2.add(32)
        ld_g = c_lb2.v

        @block.vector
        def _(ve):
            if STG < 3:
                return
            def ss(i):
                i.then_inc(s_ve, 1)
                ve.wait_ge(s_ve, c_ve.add(1))
            ve.wait_ge(s_lb2, ld_g)
            ss(ve.tensor_scalar(smg[:, 9:10], smg[:, 8:9], 0.1, 1.0, ALU.max, ALU.min))
            ss(ve.tensor_scalar(smg[:, 10:11], smg[:, 9:10], -1.0, 1.0, ALU.mult, ALU.add))
            ss(ve.tensor_tensor(smg[:, 11:12], smg[:, 10:11], smg[:, 2:3], ALU.mult))
            ss(ve.tensor_tensor(smg[:, 12:13], smg[:, 10:11], smg[:, 3:4], ALU.mult))
            ss(ve.tensor_copy(smg[:, 13:14], smg[:, 9:10]))
        ve_g = c_ve.v

        with nc.psum_tensor("psCO", [NP, 3], F32) as psCO:
          if STG >= 3:
            @block.tensor
            def _(te):
                te.wait_ge(s_ve, ve_g)
                te.matmul(psCO[:], ones_1p[:], smg[:, 11:14], start=True, stop=True).then_inc(s_pe, 1)
                c_pe.add(1)
            pe_co = c_pe.v

            @block.vector
            def _(ve):
                def ss(i):
                    i.then_inc(s_ve, 1)
                    ve.wait_ge(s_ve, c_ve.add(1))
                ve.wait_ge(s_pe, pe_co)
                ss(ve.tensor_copy(bt[:, 121:122], psCO[:, 0:1]))
                ss(ve.tensor_copy(bt[:, 122:123], psCO[:, 1:2]))
                ss(ve.tensor_copy(bt[:, 120:121], psCO[:, 2:3]))
                ss(ve.tensor_scalar(bt[:, 123:131], bt[:, 96:104], bt[:, 120:121], None, ALU.mult))
                ss(ve.tensor_scalar(bt[:, 104:112], bt[:, 104:112], bt[:, 121:122], None, ALU.mult))
                ss(ve.tensor_scalar(bt[:, 112:120], bt[:, 112:120], bt[:, 122:123], None, ALU.mult))
                ss(ve.tensor_tensor(bt[:, 123:131], bt[:, 123:131], bt[:, 104:112], ALU.add))
                ss(ve.tensor_tensor(bt[:, 123:131], bt[:, 123:131], bt[:, 112:120], ALU.add))
            ve_co = c_ve.v

        n_ln = 0 if STG < 4 else (1 if STG == 41 else 4)
        empty_store = (STG == 40)
        if STG in (40, 41):
            n_ln = 1 if STG == 41 else 4
        last_store_ve = [0]
        for s in range(n_ln):
            def src(c, _s=s):
                return xr[:, TL * c + 512 * _s:TL * c + 512 * _s + 512]

            def store(psRA, psRB, pe_bc, _s=s):
                if empty_store:
                    return

                @block.vector
                def _(ve):
                    ve.wait_ge(s_pe, pe_bc)
                    for c in range(8):
                        ve.tensor_tensor(ln.xsq[:, (c % 2) * 512:(c % 2) * 512 + 512],
                                         src(c, _s), psRA[:], ALU.mult).then_inc(s_ve, 1)
                        ve.wait_ge(s_ve, c_ve.add(1))
                        ve.tensor_tensor(xln[:, TL * c + 512 * _s:TL * c + 512 * _s + 512],
                                         ln.xsq[:, (c % 2) * 512:(c % 2) * 512 + 512],
                                         psRB[:], ALU.add).then_inc(s_ve, 1)
                        c_ve.add(1)
                    last_store_ve[0] = c_ve.v

            ln.emit_slice(src, store, pre_waits=[(s_lx, ld_x), (s_ve, ve_co)],
                          scalar_pre_waits=[(s_ve, last_store_ve[0])])
        ve_ln = c_ve.v
        se_ln = c_se.v

        pe_ln_all = c_pe.v

        @block.vector
        def _(ve):
            if STG < 5:
                return
            ve.wait_ge(s_ve, ve_co)
            ve.wait_ge(s_pe, pe_ln_all)
            for c in range(8):
                ve.tensor_scalar(xr[:, TL * c:TL * (c + 1)], xr[:, TL * c:TL * (c + 1)],
                                 bt[:, 123 + c:124 + c], None, ALU.add).then_inc(s_ve, 1)
                c_ve.add(1)
        ve_cb = c_ve.v

        paths = [(w1m, w2m, 0, 120), (w1g[0], w2g[0], 32, 121), (w1g[1], w2g[1], 64, 122)]
        lw_sem = [s_lw, s_lw1]
        lv_sem = [s_lv, s_lv1]
        pe_h_done = {}
        pe_o_done = {}

        def emit_w1_load(g):
            pi, hg = g // 4, g % 4
            slot = g % 2
            war = pe_h_done.get(g - 2)

            @block.gpsimd
            def _(gq, _p=paths[pi][0], _hg=hg, _slot=slot, _war=war):
                if _war is not None:
                    gq.wait_ge(s_pe, _war)
                for c in range(8):
                    gq.dma_start(out=w1b[:, (_slot * 8 + c) * D:(_slot * 8 + c + 1) * D],
                                 in_=_p[NP * c:NP * (c + 1), D * _hg:D * (_hg + 1)]).then_inc(
                                     lw_sem[_slot], 16)
                    c_lw[_slot].add(16)
            return (lw_sem[slot], c_lw[slot].v, slot)

        def emit_w2_load(g):
            pi, hg = g // 4, g % 4
            slot = g % 2
            war = pe_o_done.get(g - 2)

            @block.gpsimd
            def _(gq, _p=paths[pi][1], _hg=hg, _slot=slot, _war=war):
                if _war is not None:
                    gq.wait_ge(s_pe, _war)
                for c in range(8):
                    gq.dma_start(out=w2b[:, (_slot * 8 + c) * D:(_slot * 8 + c + 1) * D],
                                 in_=_p[D * _hg + NP * c:D * _hg + NP * (c + 1), :]).then_inc(
                                     lv_sem[_slot], 16)
                    c_lv[_slot].add(16)
            return (lv_sem[slot], c_lv[slot].v, slot)

        w1_pend = {0: emit_w1_load(0)}
        w2_pend = {0: emit_w2_load(0)}
        w1_pend[1] = emit_w1_load(1)
        w2_pend[1] = emit_w2_load(1)

        ve_stt_at = {}
        se_p1_done_at = {}
        for g in range(n_groups):
            pi, hg = g // 4, g % 4
            b1col, coefcol = paths[pi][2], paths[pi][3]
            (w1sem, w1cnt, w1slot) = w1_pend.pop(g)
            (w2sem, w2cnt, w2slot) = w2_pend.pop(g)

            se_h = {}
            with nc.psum_tensor("psH_%d" % g, [NP, 2 * 1024], F32) as psH:
                se_p1_0 = c_se.v
                sq_g = []
                job = 0
                for hc in range(8):
                    for tt in range(2):
                        par = job % 2

                        @block.tensor
                        def _(te, _hc=hc, _tt=tt, _par=par, _job=job):
                            if _job == 0:
                                te.wait_ge(w1sem, w1cnt)
                                te.wait_ge(s_ve, ve_ln)
                                if g == 0:
                                    te.wait_ge(s_se, se_ln)
                                if g > 0:
                                    te.wait_ge(s_ve, ve_stt_at[(g - 1, 15)])
                            if _job >= 2:
                                te.wait_ge(s_se, se_h[_job - 2])
                            for half in range(2):
                                for c in range(8):
                                    i = te.matmul(
                                        psH[:, _par * 1024 + 512 * half:_par * 1024 + 512 * (half + 1)],
                                        w1b[:, (w1slot * 8 + c) * D + NP * _hc:
                                            (w1slot * 8 + c) * D + NP * (_hc + 1)],
                                        xln[:, TL * c + 1024 * _tt + 512 * half:
                                            TL * c + 1024 * _tt + 512 * (half + 1)],
                                        start=(c == 0), stop=(c == 7))
                            i.then_inc(s_pe, 1)
                            c_pe.add(1)
                        pe_j = c_pe.v

                        def fgel(se, _hc=hc, _tt=tt, _par=par, _pe=pe_j, _job=job,
                                 _g=g, _b1=b1col, _hg=hg):
                            se.wait_ge(s_pe, _pe)
                            if _g == 0 and _job == 0:
                                se.wait_ge(s_ve, ve_co)
                            se.activation(
                                hblk[:, TL * _hc + 1024 * _tt:TL * _hc + 1024 * (_tt + 1)],
                                psH[:, _par * 1024:(_par + 1) * 1024],
                                AF.Gelu_apprx_tanh,
                                bias=bt[:, _b1 + _hg * 8 + _hc:_b1 + _hg * 8 + _hc + 1]
                            ).then_inc(s_se, 1)
                        sq_g.append(fgel)
                        c_se.add(1)
                        se_h[job] = c_se.v
                        job += 1

                @block.scalar
                def _(se, _sq=tuple(sq_g)):
                    for f in _sq:
                        f(se)
                pe_h_done[g] = c_pe.v
                se_p1_done = c_se.v
                se_p1_done_at[g] = se_p1_done

            if g + 2 < n_groups:
                w1_pend[g + 2] = emit_w1_load(g + 2)

            with nc.psum_tensor("psO_%d" % g, [NP, 2 * 1024], F32) as psO:
                job = 0
                for tt in range(2):
                    for dc in range(8):
                        par = job % 2

                        @block.tensor
                        def _(te, _tt=tt, _dc=dc, _par=par, _job=job):
                            if _job == 0:
                                te.wait_ge(w2sem, w2cnt)
                                te.wait_ge(s_se, se_p1_done)
                            if _job >= 2:
                                te.wait_ge(s_ve, ve_stt_at[(g, _job - 2)])
                            for half in range(2):
                                for hc in range(8):
                                    i = te.matmul(
                                        psO[:, _par * 1024 + 512 * half:_par * 1024 + 512 * (half + 1)],
                                        w2b[:, (w2slot * 8 + hc) * D + NP * _dc:
                                            (w2slot * 8 + hc) * D + NP * (_dc + 1)],
                                        hblk[:, TL * hc + 1024 * _tt + 512 * half:
                                            TL * hc + 1024 * _tt + 512 * (half + 1)],
                                        start=(hc == 0), stop=(hc == 7))
                            i.then_inc(s_pe, 1)
                            c_pe.add(1)
                        pe_j = c_pe.v

                        @block.vector
                        def _(ve, _tt=tt, _dc=dc, _par=par, _pe=pe_j, _job=job):
                            ve.wait_ge(s_pe, _pe)
                            if g == 0 and _job == 0:
                                ve.wait_ge(s_ve, ve_cb)
                            dst = xr[:, TL * _dc + 1024 * _tt:TL * _dc + 1024 * (_tt + 1)]
                            ve.scalar_tensor_tensor(
                                dst, psO[:, _par * 1024:(_par + 1) * 1024],
                                bt[:, coefcol:coefcol + 1], dst,
                                ALU.mult, ALU.add).then_inc(s_ve, 1)
                            c_ve.add(1)
                        ve_stt_at[(g, job)] = c_ve.v
                        job += 1
                pe_o_done[g] = c_pe.v

            if g + 2 < n_groups:
                w2_pend[g + 2] = emit_w2_load(g + 2)

        ve_fin = c_ve.v

        @block.sync
        def _(sync):
            sync.wait_ge(s_ve, ve_fin)
            for c in range(8):
                sync.dma_start(out=x2[NP * c:NP * (c + 1), :],
                               in_=xr[:, TL * c:TL * (c + 1)]).then_inc(s_st, 16)
                c_st.add(16)
            sync.wait_ge(s_st, c_st.v)
    return nc



def build_l1_attn():
    nc = bass.Bass("TRN2", target_bir_lowering=False)
    x2f = nc.dram_tensor("x2f", [D, T], F32, kind="ExternalInput").ap()
    p_ire = nc.dram_tensor("p_ire", [NP, 4], F32, kind="ExternalInput").ap()
    p_iim = nc.dram_tensor("p_iim", [NP, 4], F32, kind="ExternalInput").ap()
    p_ang = nc.dram_tensor("p_ang", [NP, 4], F32, kind="ExternalInput").ap()
    p_lml = nc.dram_tensor("p_lml", [NP, 4], F32, kind="ExternalInput").ap()
    p_lcr = nc.dram_tensor("p_lcr", [NP, 4], F32, kind="ExternalInput").ap()
    p_lci = nc.dram_tensor("p_lci", [NP, 4], F32, kind="ExternalInput").ap()
    p_lsc = nc.dram_tensor("p_lsc", [NP, 4], F32, kind="ExternalInput").ap()
    x3p = nc.dram_tensor("x3p", [4 * NP, T], F32, kind="ExternalOutput").ap()

    W = 1024
    NTS = T // W
    INV2PI = -1.0 / (2.0 * np.pi)
    TWOPI = 2.0 * np.pi

    with ExitStack() as ctx:
        E = ctx.enter_context
        xb = E(nc.sbuf_tensor("xb", [NP, 2 * 8 * W], F32))
        pp = E(nc.sbuf_tensor("pp", [NP, 16 * 4], F32))
        wk = E(nc.sbuf_tensor("wk", [NP, 10 * W], F32))
        wki = E(nc.sbuf_tensor("wki", [NP, W], I32))
        tio = E(nc.sbuf_tensor("tio", [NP, W], I32))
        pi2t = E(nc.sbuf_tensor("pi2t", [NP, 1], F32))
        bsin = E(nc.sbuf_tensor("bsin", [NP, 1], F32))
        bcos = E(nc.sbuf_tensor("bcos", [NP, 1], F32))
        thsw = E(nc.sbuf_tensor("thsw", [NP, 1], F32))
        xout = E(nc.sbuf_tensor("xout", [NP, W], F32))
        s_ld = E(nc.semaphore("s_ld"))
        s_lp = E(nc.semaphore("s_lp"))
        s_pp = E(nc.semaphore("s_pp"))
        s_pe = E(nc.semaphore("s_pe"))
        s_se = E(nc.semaphore("s_se"))
        s_ve = E(nc.semaphore("s_ve"))
        s_gp = E(nc.semaphore("s_gp"))
        s_st = E(nc.semaphore("s_st"))
        block = E(nc.Block())
        c_ld, c_lp, c_pp, c_pe, c_se, c_ve, c_gp, c_st = (Ctr() for _ in range(8))
        ln = LNEnv(nc, ctx, block, (s_pe, s_se, s_ve), (c_pe, c_se, c_ve), "m", maxw=W)
        ln.emit_consts()

        def vss(ve):
            def ss(i):
                i.then_inc(s_ve, 1)
                ve.wait_ge(s_ve, c_ve.add(1))
            return ss

        @block.sync
        def _(sync):
            for (t, ap) in [(0, p_ire), (1, p_iim), (2, p_ang), (3, p_lml),
                            (4, p_lcr), (5, p_lci), (6, p_lsc)]:
                sync.dma_start(out=pp[:].rearrange("p (a b) -> p a b", b=16)[:, :, t:t + 1],
                               in_=ap.unsqueeze(2), allow_slow_non_contiguous=True).then_inc(s_pp, 16)
                c_pp.add(16)
        ld_p = c_pp.v

        @block.gpsimd
        def _(g):
            g.iota(tio[:], pattern=[[1, W]], base=0, channel_multiplier=0).then_inc(s_gp, 1)
            c_gp.add(1)

        PC = {k: i for i, k in enumerate(
            "ire iim ang lml lcr lci lsc theta deff r gate costh sinth nire cre cim".split())}

        def pcol(j, name):
            return pp[:, 16 * j + PC[name]:16 * j + PC[name] + 1]

        @block.scalar
        def _(se):
            se.wait_ge(s_pp, ld_p)
            for j in range(4):
                se.activation(pcol(j, "deff"), pcol(j, "lml"), AF.Exp).then_inc(s_se, 1)
                c_se.add(1)
        se_deff = c_se.v

        @block.vector
        def _(ve):
            ss = vss(ve)
            ve.wait_ge(s_se, se_deff)
            ss(ve.memset(pi2t[:], 1.5707963267948966))
            ss(ve.memset(bsin[:], -6.283185307179586))
            ss(ve.memset(bcos[:], 1.5707963267948966 - 6.283185307179586))
            for j in range(4):
                ss(ve.tensor_tensor(pcol(j, "deff"), pcol(j, "deff"), pcol(j, "lsc"), ALU.mult))
                ss(ve.tensor_tensor(pcol(j, "theta"), pcol(j, "ang"), pcol(j, "lsc"), ALU.mult))
                ss(ve.tensor_scalar(pcol(j, "r"), pcol(j, "deff"), -1.0, None, ALU.mult))
                ss(ve.tensor_scalar(pcol(j, "nire"), pcol(j, "ire"), -1.0, None, ALU.mult))
        ve_p1 = c_ve.v

        @block.scalar
        def _(se):
            se.wait_ge(s_ve, ve_p1)
            for j in range(4):
                se.activation(pcol(j, "r"), pcol(j, "r"), AF.Exp).then_inc(s_se, 1)
                c_se.add(1)
                se.activation(pcol(j, "sinth"), pcol(j, "theta"), AF.Sin).then_inc(s_se, 1)
                c_se.add(1)
                se.activation(pcol(j, "costh"), pcol(j, "theta"), AF.Sin,
                              bias=pi2t[:]).then_inc(s_se, 1)
                c_se.add(1)
        se_p2 = c_se.v

        @block.vector
        def _(ve):
            ss = vss(ve)
            ve.wait_ge(s_se, se_p2)
            for j in range(4):
                ss(ve.tensor_scalar(pcol(j, "gate"), pcol(j, "r"), -1.0, 1.0, ALU.mult, ALU.add))
                ss(ve.tensor_tensor(pcol(j, "cre"), pcol(j, "costh"), pcol(j, "lcr"), ALU.mult))
                ss(ve.tensor_tensor(wk[:, 0:1], pcol(j, "sinth"), pcol(j, "lci"), ALU.mult))
                ss(ve.tensor_tensor(pcol(j, "cre"), pcol(j, "cre"), wk[:, 0:1], ALU.subtract))
                ss(ve.tensor_tensor(pcol(j, "cim"), pcol(j, "sinth"), pcol(j, "lcr"), ALU.mult))
                ss(ve.tensor_tensor(wk[:, 0:1], pcol(j, "costh"), pcol(j, "lci"), ALU.mult))
                ss(ve.tensor_tensor(pcol(j, "cim"), pcol(j, "cim"), wk[:, 0:1], ALU.add))
        ve_pc = c_ve.v

        PH, RD, SN, CS, AB, VR, VI, ZR, ZI, TMP = range(10)

        def w(slot):
            return wk[:, W * slot:W * (slot + 1)]

        ve_done_slice = {}
        for s in range(NTS):
            slot = s % 2
            sem = s_ld if slot == 0 else s_lp
            ctr2 = c_ld if slot == 0 else c_lp
            war = ve_done_slice.get(s - 2)

            @block.sync
            def _(sync, _s=s, _slot=slot, _war=war, _sem=sem):
                if _war is not None:
                    sync.wait_ge(s_ve, _war)
                for c in range(8):
                    sync.dma_start(out=xb[:, (_slot * 8 + c) * W:(_slot * 8 + c + 1) * W],
                                   in_=x2f[NP * c:NP * (c + 1), W * _s:W * (_s + 1)]).then_inc(_sem, 16)
                    ctr2.add(16)
            ld_this = (sem, ctr2.v)

            def src(c, _slot=slot):
                return xb[:, (_slot * 8 + c) * W:(_slot * 8 + c + 1) * W]

            def store2(psRA, psRB, pe_bc, _s=s, _slot=slot):
                @block.vector
                def _(ve):
                    ve.wait_ge(s_pe, pe_bc)
                    if _s == 0:
                        ve.wait_ge(s_ve, ve_pc)
                        ve.wait_ge(s_gp, c_gp.v)
                for j in range(4):
                    @block.vector
                    def _(ve, _j=j):
                        ss = vss(ve)
                        ss(ve.tensor_tensor(w(TMP), src(_j, _slot), psRA[:], ALU.mult))
                        ss(ve.tensor_tensor(w(TMP), w(TMP), psRB[:], ALU.add))
                        ss(ve.tensor_scalar(w(PH), tio[:], pcol(_j, "theta"), None, ALU.mult))
                        if _s > 0:
                            ss(ve.tensor_scalar(thsw[:], pcol(_j, "theta"), float(_s * W),
                                                None, ALU.mult))
                            ss(ve.tensor_scalar(w(PH), w(PH), thsw[:], None, ALU.add))
                        ss(ve.tensor_scalar(w(RD), w(PH), INV2PI, None, ALU.mult))
                        ss(ve.tensor_scalar(wki[:], w(RD), 1.0, None, ALU.mult))
                        ss(ve.tensor_copy(w(SN), wki[:]))
                        ss(ve.scalar_tensor_tensor(w(CS), w(SN), -1.0, w(RD), ALU.mult, ALU.add))
                        ss(ve.scalar_tensor_tensor(w(SN), w(CS), 0.5, w(SN), ALU.is_ge, ALU.add))
                        ss(ve.scalar_tensor_tensor(w(SN), w(CS), -0.5, w(SN), ALU.is_ge, ALU.add))
                        ss(ve.scalar_tensor_tensor(w(RD), w(SN), TWOPI, w(PH), ALU.mult, ALU.add))
                        ss(ve.tensor_scalar(w(CS), w(PH), INV2PI, -0.25, ALU.mult, ALU.add))
                        ss(ve.tensor_scalar(wki[:], w(CS), 1.0, None, ALU.mult))
                        ss(ve.tensor_copy(w(SN), wki[:]))
                        ss(ve.scalar_tensor_tensor(w(AB), w(SN), -1.0, w(CS), ALU.mult, ALU.add))
                        ss(ve.scalar_tensor_tensor(w(SN), w(AB), 0.5, w(SN), ALU.is_ge, ALU.add))
                        ss(ve.scalar_tensor_tensor(w(SN), w(AB), -0.5, w(SN), ALU.is_ge, ALU.add))
                        ss(ve.scalar_tensor_tensor(w(CS), w(SN), TWOPI, w(PH), ALU.mult, ALU.add))
                    ve_pre = c_ve.v

                    @block.scalar
                    def _(se, _ve=ve_pre):
                        se.wait_ge(s_ve, _ve)
                        se.activation(w(SN), w(RD), AF.Sin, bias=bsin[:]).then_inc(s_se, 1)
                        c_se.add(1)
                        se.activation(w(CS), w(CS), AF.Sin, bias=bcos[:]).then_inc(s_se, 1)
                        c_se.add(1)
                    se_sc = c_se.v

                    @block.vector
                    def _(ve, _j=j, _se=se_sc):
                        ss = vss(ve)
                        ve.wait_ge(s_se, _se)
                        ss(ve.tensor_scalar(w(AB), w(CS), pcol(_j, "ire"), None, ALU.mult))
                        ss(ve.scalar_tensor_tensor(w(VR), w(SN), pcol(_j, "iim"), w(AB),
                                                   ALU.mult, ALU.add))
                        ss(ve.tensor_tensor(w(VR), w(VR), w(TMP), ALU.mult))
                        ss(ve.tensor_scalar(w(AB), w(CS), pcol(_j, "iim"), None, ALU.mult))
                        ss(ve.scalar_tensor_tensor(w(VI), w(SN), pcol(_j, "nire"), w(AB),
                                                   ALU.mult, ALU.add))
                        ss(ve.tensor_tensor(w(VI), w(VI), w(TMP), ALU.mult))
                        ss(ve.tensor_tensor_scan(w(ZR), pcol(_j, "r").broadcast_to([NP, W]),
                                                 w(VR), pcol(_j, "cre"), ALU.mult, ALU.add))
                        ss(ve.tensor_tensor_scan(w(ZI), pcol(_j, "r").broadcast_to([NP, W]),
                                                 w(VI), pcol(_j, "cim"), ALU.mult, ALU.add))
                        ss(ve.tensor_copy(pcol(_j, "cre"), w(ZR)[:, W - 1:W]))
                        ss(ve.tensor_copy(pcol(_j, "cim"), w(ZI)[:, W - 1:W]))
                        ss(ve.tensor_tensor(w(AB), w(CS), w(ZR), ALU.mult))
                        ss(ve.tensor_tensor(w(RD), w(SN), w(ZI), ALU.mult))
                        ss(ve.tensor_tensor(w(AB), w(AB), w(RD), ALU.subtract))
                        ss(ve.tensor_scalar(w(AB), w(AB), pcol(_j, "gate"), None, ALU.mult))
                        ss(ve.tensor_tensor(xout[:], src(_j, _slot), w(AB), ALU.add))
                    ve_y = c_ve.v

                    @block.sync
                    def _(sync, _j=j, _ve=ve_y):
                        sync.wait_ge(s_ve, _ve)
                        sync.dma_start(out=x3p[NP * _j:NP * (_j + 1), W * _s:W * (_s + 1)],
                                       in_=xout[:]).then_inc(s_st, 16)
                        c_st.add(16)
                    st_now = c_st.v

                    @block.vector
                    def _(ve, _st=st_now):
                        ve.wait_ge(s_st, _st)
                ve_done_slice[_s] = c_ve.v

            ln.emit_slice(src, store2, pre_waits=[ld_this],
                          scalar_pre_waits=[(s_ve, ve_done_slice.get(s - 1, 0))],
                          width=W)

        @block.sync
        def _(sync):
            sync.wait_ge(s_st, c_st.v)
    return nc



def emit_round_chain(ve, ss, y, ki, rf, d):
    ss(ve.tensor_scalar(ki, y, 1.0, None, ALU.mult))
    ss(ve.tensor_copy(rf, ki))
    ss(ve.scalar_tensor_tensor(d, rf, -1.0, y, ALU.mult, ALU.add))
    ss(ve.scalar_tensor_tensor(rf, d, 0.5, rf, ALU.is_ge, ALU.add))
    ss(ve.scalar_tensor_tensor(rf, d, -0.5, rf, ALU.is_ge, ALU.add))



def build_l1_q1():
    nc = bass.Bass("TRN2", target_bir_lowering=False)
    x3 = nc.dram_tensor("x3", [D, TL], F32, kind="ExternalInput").ap()
    m1 = nc.dram_tensor("m1", [1, 1], F32, kind="ExternalOutput").ap()
    with ExitStack() as ctx:
        E = ctx.enter_context
        xb = E(nc.sbuf_tensor("xb", [NP, 8 * TL], F32))
        tsc = E(nc.sbuf_tensor("tsc", [NP, 512], F32))
        macc = E(nc.sbuf_tensor("macc", [NP, 10], F32))
        id128 = E(nc.sbuf_tensor("id128", [NP, NP], F32))
        id128i = E(nc.sbuf_tensor("id128i", [NP, NP], I32))
        fin = E(nc.sbuf_tensor("fin", [1, 2], F32))
        s_ld = E(nc.semaphore("s_ld"))
        s_pe = E(nc.semaphore("s_pe"))
        s_se = E(nc.semaphore("s_se"))
        s_ve = E(nc.semaphore("s_ve"))
        s_gp = E(nc.semaphore("s_gp"))
        s_st = E(nc.semaphore("s_st"))
        block = E(nc.Block())
        c_ld, c_pe, c_se, c_ve, c_gp, c_st = (Ctr() for _ in range(6))
        ln = LNEnv(nc, ctx, block, (s_pe, s_se, s_ve), (c_pe, c_se, c_ve), "q")
        ln.emit_consts()

        @block.sync
        def _(sync):
            for c in range(8):
                sync.dma_start(out=xb[:, TL * c:TL * (c + 1)],
                               in_=x3[NP * c:NP * (c + 1), :]).then_inc(s_ld, 16)
                c_ld.add(16)
        ld_x = c_ld.v

        @block.gpsimd
        def _(g):
            g.iota(id128i[:], pattern=[[-1, NP]], base=0, channel_multiplier=1).then_inc(s_gp, 1)
            c_gp.add(1)

        @block.vector
        def _(ve):
            def ss(i):
                i.then_inc(s_ve, 1)
                ve.wait_ge(s_ve, c_ve.add(1))
            ve.wait_ge(s_gp, c_gp.v)
            ss(ve.memset(macc[:], 0.0))
            ss(ve.tensor_copy(id128[:], id128i[:]))
            ss(ve.tensor_single_scalar(id128[:], id128[:], 0.0, ALU.is_equal))
        ve_id = c_ve.v

        for s in range(4):
            def src(c, _s=s):
                return xb[:, TL * c + 512 * _s:TL * c + 512 * _s + 512]

            def store(psRA, psRB, pe_bc, _s=s):
                @block.vector
                def _(ve):
                    def ss(i):
                        i.then_inc(s_ve, 1)
                        ve.wait_ge(s_ve, c_ve.add(1))
                    ve.wait_ge(s_pe, pe_bc)
                    if _s == 0:
                        ve.wait_ge(s_ve, ve_id)
                    for c in range(8):
                        ss(ve.tensor_tensor(tsc[:], src(c, _s), psRA[:], ALU.mult))
                        ss(ve.tensor_tensor(tsc[:], tsc[:], psRB[:], ALU.add))
                        ss(ve.reduce_max(macc[:, 9:10], tsc[:], axis=AX.X,
                                         apply_absolute_value=True))
                        ss(ve.tensor_tensor(macc[:, 0:1], macc[:, 0:1], macc[:, 9:10], ALU.max))

            ln.emit_slice(src, store, pre_waits=[(s_ld, ld_x)])
        ve_acc = c_ve.v

        with nc.psum_tensor("psT", [1, NP], F32) as psT:
            @block.tensor
            def _(te):
                te.wait_ge(s_ve, ve_acc)
                te.transpose(psT[:], macc[:, 0:1], id128[:]).then_inc(s_pe, 1)
                c_pe.add(1)
            pe_t = c_pe.v

            @block.vector
            def _(ve):
                ve.wait_ge(s_pe, pe_t)
                ve.reduce_max(fin[:, 0:1], psT[:], axis=AX.X).then_inc(s_ve, 1)
                c_ve.add(1)
            ve_f = c_ve.v

        @block.sync
        def _(sync):
            sync.wait_ge(s_ve, ve_f)
            sync.dma_start(out=m1, in_=fin[:, 0:1]).then_inc(s_st, 16)
            c_st.add(16)
            sync.wait_ge(s_st, c_st.v)
    return nc



def build_l1_q2():
    nc = bass.Bass("TRN2", target_bir_lowering=False)
    x3 = nc.dram_tensor("x3", [D, TL], F32, kind="ExternalInput").ap()
    m1g = nc.dram_tensor("m1g", [1, 8], F32, kind="ExternalInput").ap()
    wup = nc.dram_tensor("wup", [3 * D, D], F32, kind="ExternalInput").ap()
    wgt = nc.dram_tensor("wgt", [3 * D, D], F32, kind="ExternalInput").ap()
    upr = nc.dram_tensor("upr", [3 * D, TL], F32, kind="ExternalOutput").ap()
    gtr = nc.dram_tensor("gtr", [3 * D, TL], F32, kind="ExternalOutput").ap()
    mm = nc.dram_tensor("mm", [1, 2], F32, kind="ExternalOutput").ap()
    wq_dram = nc.dram_tensor("wq_dram", [3 * D, D], BF16).ap()

    NI = 3 * D // NP

    with ExitStack() as ctx:
        E = ctx.enter_context
        xb = E(nc.sbuf_tensor("xb", [NP, 8 * TL], F32))
        xq = E(nc.sbuf_tensor("xq", [NP, 8 * TL], BF16))
        wbuf = E(nc.sbuf_tensor("wbuf", [NP, 2 * D], F32))
        wqb = E(nc.sbuf_tensor("wqb", [NP, 2 * D], BF16))
        wqT = E(nc.sbuf_tensor("wqT", [NP, 8 * 3 * D], BF16))
        wsc = E(nc.sbuf_tensor("wsc", [NP, 8 * 32], F32))
        tsc = E(nc.sbuf_tensor("tsc", [NP, 3 * 1024], F32))
        tki = E(nc.sbuf_tensor("tki", [NP, 1024], I32))
        ev = E(nc.sbuf_tensor("ev", [NP, 2 * 512], F32))
        macc = E(nc.sbuf_tensor("macc", [NP, 4], F32))
        id128 = E(nc.sbuf_tensor("id128", [NP, NP], F32))
        id128i = E(nc.sbuf_tensor("id128i", [NP, NP], I32))
        fin = E(nc.sbuf_tensor("fin", [1, 8], F32))
        scol = E(nc.sbuf_tensor("scol", [NP, 2], F32))
        ivcol = E(nc.sbuf_tensor("ivcol", [NP, 1], F32))
        s_ld = E(nc.semaphore("s_ld"))
        s_lm = E(nc.semaphore("s_lm"))
        s_lw0 = E(nc.semaphore("s_lw0"))
        s_lw1 = E(nc.semaphore("s_lw1"))
        s_wq = E(nc.semaphore("s_wq"))
        s_tq = E(nc.semaphore("s_tq"))
        s_pe = E(nc.semaphore("s_pe"))
        s_se = E(nc.semaphore("s_se"))
        s_ve = E(nc.semaphore("s_ve"))
        s_gp = E(nc.semaphore("s_gp"))
        s_gm = E(nc.semaphore("s_gm"))
        s_st0 = E(nc.semaphore("s_st0"))
        s_st1 = E(nc.semaphore("s_st1"))
        s_st = E(nc.semaphore("s_st"))
        block = E(nc.Block())
        c_ld, c_lm, c_wq, c_tq, c_pe, c_se, c_ve, c_gp, c_st = (Ctr() for _ in range(9))
        c_lw = [Ctr(), Ctr()]
        c_sto = [Ctr(), Ctr()]
        c_gm = Ctr()
        ln = LNEnv(nc, ctx, block, (s_pe, s_se, s_ve), (c_pe, c_se, c_ve), "q2")
        ln.emit_consts()

        @block.sync
        def _(sync):
            for c in range(8):
                sync.dma_start(out=xb[:, TL * c:TL * (c + 1)],
                               in_=x3[NP * c:NP * (c + 1), :]).then_inc(s_ld, 16)
                c_ld.add(16)
        ld_x = c_ld.v

        @block.sync
        def _(sync):
            sync.dma_start(out=fin[:], in_=m1g).then_inc(s_lm, 16)
            c_lm.add(16)
        ld_m = c_lm.v

        @block.gpsimd
        def _(g):
            g.iota(id128i[:], pattern=[[-1, NP]], base=0, channel_multiplier=1).then_inc(s_gp, 1)
            c_gp.add(1)

        @block.vector
        def _(ve):
            def ss(i):
                i.then_inc(s_ve, 1)
                ve.wait_ge(s_ve, c_ve.add(1))
            ve.wait_ge(s_gp, c_gp.v)
            ve.wait_ge(s_lm, ld_m)
            ss(ve.memset(macc[:], 0.0))
            ss(ve.tensor_copy(id128[:], id128i[:]))
            ss(ve.tensor_single_scalar(id128[:], id128[:], 0.0, ALU.is_equal))
            ss(ve.reduce_max(fin[:, 0:1] if False else macc[0:1, 3:4], fin[:], axis=AX.X))
            ss(ve.tensor_scalar(macc[0:1, 1:2], macc[0:1, 3:4], 1.0 / 32767.0, 1e-12,
                                ALU.mult, ALU.max))
            ss(ve.reciprocal(macc[0:1, 2:3], macc[0:1, 1:2]))
        ve_s1 = c_ve.v

        with nc.psum_tensor("psBC", [NP, 2], F32) as psBC:
            @block.tensor
            def _(te):
                te.wait_ge(s_ve, ve_s1)
                te.matmul(psBC[:], ln.ones_1p[:], macc[0:1, 1:3], start=True,
                          stop=True).then_inc(s_pe, 1)
                c_pe.add(1)
            pe_bc2 = c_pe.v

            @block.vector
            def _(ve):
                def ss(i):
                    i.then_inc(s_ve, 1)
                    ve.wait_ge(s_ve, c_ve.add(1))
                ve.wait_ge(s_pe, pe_bc2)
                ss(ve.tensor_copy(scol[:, 0:1], psBC[:, 0:1]))
                ss(ve.tensor_scalar(scol[:, 1:2], psBC[:, 0:1], -1.0, None, ALU.mult))
                ss(ve.tensor_copy(ivcol[:], psBC[:, 1:2]))
        ve_bc = c_ve.v

        for s in range(4):
            def src(c, _s=s):
                return xb[:, TL * c + 512 * _s:TL * c + 512 * _s + 512]

            def store(psRA, psRB, pe_bc, _s=s):
                @block.vector
                def _(ve):
                    def ss(i):
                        i.then_inc(s_ve, 1)
                        ve.wait_ge(s_ve, c_ve.add(1))
                    ve.wait_ge(s_pe, pe_bc)
                    if _s == 0:
                        ve.wait_ge(s_ve, ve_bc)
                    for c in range(8):
                        t0 = tsc[:, 0:512]; t1 = tsc[:, 512:1024]; t2 = tsc[:, 1024:1536]
                        ss(ve.tensor_tensor(t0, src(c, _s), psRA[:], ALU.mult))
                        ss(ve.tensor_tensor(t0, t0, psRB[:], ALU.add))
                        ss(ve.tensor_scalar(t1, t0, ivcol[:], None, ALU.mult))
                        emit_round_chain(ve, ss, t1, tki[:, 0:512], t2, t0)
                        ss(ve.tensor_scalar(xq[:, TL * c + 512 * _s:TL * c + 512 * _s + 512],
                                            t2, scol[:, 0:1], scol[:, 1:2], ALU.mult, ALU.add))

            ln.emit_slice(src, store, pre_waits=[(s_ld, ld_x), (s_ve, ve_bc)])
        ve_xq = c_ve.v
        se_xq = c_se.v

        lw_sem = [s_lw0, s_lw1]
        st_sem = [s_st0, s_st1]
        out_t = [upr, gtr]
        w_in = [wup, wgt]
        ve_dq_at = {}

        pe_mm_done = {}
        st_prev = {}
        gm_prev = {}
        se_ev_at = {}
        for wt in range(2):
            for r in range(NI):
                slot = r % 2
                war = ve_dq_at.get((wt, r - 2), ve_dq_at.get((wt - 1, NI - 2 + r)))
                war_tq = c_tq.v if (wt == 1 and r == 0) else None

                @block.sync
                def _(sync, _wt=wt, _r=r, _slot=slot, _war=war):
                    if _war is not None:
                        sync.wait_ge(s_ve, _war)
                    sync.dma_start(out=wbuf[:, D * _slot:D * (_slot + 1)],
                                   in_=w_in[_wt][NP * _r:NP * (_r + 1), :]).then_inc(
                                       lw_sem[_slot], 16)
                    c_lw[_slot].add(16)
                ld_w = (lw_sem[slot], c_lw[slot].v)

                @block.vector
                def _(ve, _wt=wt, _r=r, _slot=slot, _ld=ld_w,
                      _wq=(c_wq.v if (wt * NI + r) >= 2 else None)):
                    def ss(i):
                        i.then_inc(s_ve, 1)
                        ve.wait_ge(s_ve, c_ve.add(1))
                    ve.wait_ge(_ld[0], _ld[1])
                    if _wq is not None:
                        ve.wait_ge(s_wq, _wq)
                    wv = wbuf[:, D * _slot:D * (_slot + 1)]
                    w3 = wv.rearrange("p (a b) -> p a b", b=32)
                    mx = wsc[:, 0:32]; sc_ = wsc[:, 32:64]; rc = wsc[:, 64:96]
                    ss(ve.reduce_max(mx, w3, axis=AX.X, apply_absolute_value=True))
                    ss(ve.tensor_scalar(sc_, mx, 1.0 / 7.0, 1e-12, ALU.mult, ALU.max))
                    ss(ve.reciprocal(rc, sc_))
                    rch = rc[:, 0:32].unsqueeze(2).broadcast_to([NP, 32, 32])
                    sch = sc_[:, 0:32].unsqueeze(2).broadcast_to([NP, 32, 32])
                    y3 = tsc[:, 0:1024].rearrange("p (a b) -> p a b", b=32)
                    ss(ve.tensor_tensor(y3, w3, rch, ALU.mult))
                    emit_round_chain(ve, ss, tsc[:, 0:1024], tki[:],
                                     tsc[:, 1024:2048], tsc[:, 2048:3072])
                    r3 = tsc[:, 1024:2048].rearrange("p (a b) -> p a b", b=32)
                    u3 = tsc[:, 2048:3072].rearrange("p (a b) -> p a b", b=32)
                    ss(ve.tensor_tensor(u3, r3, sch, ALU.mult))
                    q3 = wqb[:, D * _slot:D * (_slot + 1)].rearrange(
                        "p (a b) -> p a b", b=32)
                    ss(ve.tensor_tensor(q3, u3, sch, ALU.subtract))
                ve_dq_at[(wt, r)] = c_ve.v

                @block.sync
                def _(sync, _wt=wt, _r=r, _slot=slot, _ve=ve_dq_at[(wt, r)],
                      _wtq=war_tq):
                    if _wtq is not None:
                        sync.wait_ge(s_tq, _wtq)
                    sync.wait_ge(s_ve, _ve)
                    sync.dma_start(out=wq_dram[NP * _r:NP * (_r + 1), :],
                                   in_=wqb[:, D * _slot:D * (_slot + 1)]).then_inc(s_wq, 16)
                    c_wq.add(16)
            wq_done = c_wq.v

            @block.sync
            def _(sync, _wq=wq_done, _pmm=pe_mm_done.get(wt - 1)):
                sync.wait_ge(s_wq, _wq)
                if _pmm is not None:
                    sync.wait_ge(s_pe, _pmm)
                for c in range(8):
                    sync.dma_start(out=wqT[:, 3 * D * c:3 * D * (c + 1)],
                                   in_=wq_dram[:, NP * c:NP * (c + 1)],
                                   transpose=True).then_inc(s_tq, 16)
                    c_tq.add(16)
            tq_done = c_tq.v

            sq_w = []
            with nc.psum_tensor("psU_%d" % wt, [NP, 2 * 512], F32) as psU:
                for ic in range(NI):
                    for tt in range(4):
                        job = wt * NI * 4 + ic * 4 + tt
                        par = job % 2

                        @block.tensor
                        def _(te, _ic=ic, _tt=tt, _par=par, _job=job, _tq=tq_done, _wt=wt):
                            if _job % (NI * 4) == 0:
                                te.wait_ge(s_tq, _tq)
                                te.wait_ge(s_ve, ve_xq)
                                if _wt == 0:
                                    te.wait_ge(s_se, se_xq)
                            if _job >= 2:
                                te.wait_ge(s_se, se_ev_at[_job - 2])
                            for c in range(8):
                                i = te.matmul(psU[:, 512 * _par:512 * (_par + 1)],
                                              wqT[:, 3 * D * c + NP * _ic:3 * D * c + NP * (_ic + 1)],
                                              xq[:, TL * c + 512 * _tt:TL * c + 512 * (_tt + 1)],
                                              start=(c == 0), stop=(c == 7))
                            i.then_inc(s_pe, 1)
                            c_pe.add(1)
                        pe_u = c_pe.v

                        def fse(se, _par=par, _pe=pe_u, _war=st_prev.get(par),
                                _gwar=gm_prev.get(par)):
                            se.wait_ge(s_pe, _pe)
                            if _war is not None:
                                se.wait_ge(st_sem[_par], _war)
                            if _gwar is not None:
                                se.wait_ge(s_ve, _gwar)
                            se.copy(ev[:, 512 * _par:512 * (_par + 1)],
                                    psU[:, 512 * _par:512 * (_par + 1)]).then_inc(s_se, 1)
                        sq_w.append(fse)
                        c_se.add(1)
                        se_ev = c_se.v
                        se_ev_at[job] = c_se.v

                        @block.vector
                        def _(ve, _par=par, _se=se_ev, _wt=wt):
                            def ss(i):
                                i.then_inc(s_ve, 1)
                                ve.wait_ge(s_ve, c_ve.add(1))
                            ve.wait_ge(s_se, _se)
                            ss(ve.reduce_max(macc[:, 3:4], ev[:, 512 * _par:512 * (_par + 1)],
                                             axis=AX.X, apply_absolute_value=True))
                            ss(ve.tensor_tensor(macc[:, _wt:_wt + 1], macc[:, _wt:_wt + 1],
                                                macc[:, 3:4], ALU.max))
                        gm_prev[par] = c_ve.v

                        @block.sync
                        def _(sync, _ic=ic, _tt=tt, _par=par, _se=se_ev, _wt=wt):
                            sync.wait_ge(s_se, _se)
                            sync.dma_start(out=out_t[_wt][NP * _ic:NP * (_ic + 1),
                                                          512 * _tt:512 * (_tt + 1)],
                                           in_=ev[:, 512 * _par:512 * (_par + 1)]).then_inc(
                                               st_sem[_par], 16)
                            c_sto[_par].add(16)
                        st_prev[par] = c_sto[par].v

                @block.scalar
                def _(se, _sq=tuple(sq_w)):
                    for f in _sq:
                        f(se)
            pe_mm_done[wt] = c_pe.v

        ve_all = c_ve.v
        with nc.psum_tensor("psT2", [1, NP], F32) as psT:
            for wt in range(2):
                @block.tensor
                def _(te, _wt=wt):
                    if _wt == 0:
                        te.wait_ge(s_ve, ve_all)
                    te.transpose(psT[:], macc[:, _wt:_wt + 1], id128[:]).then_inc(s_pe, 1)
                    c_pe.add(1)
                pe_t = c_pe.v

                @block.vector
                def _(ve, _wt=wt, _pe=pe_t):
                    ve.wait_ge(s_pe, _pe)
                    ve.reduce_max(fin[:, 4 + _wt:5 + _wt], psT[:], axis=AX.X).then_inc(s_ve, 1)
                    c_ve.add(1)
                ve_f = c_ve.v

                @block.tensor
                def _(te, _ve=ve_f):
                    te.wait_ge(s_ve, _ve)
        ve_fin = c_ve.v

        @block.sync
        def _(sync):
            sync.wait_ge(s_ve, ve_fin)
            sync.wait_ge(s_st0, c_sto[0].v)
            sync.wait_ge(s_st1, c_sto[1].v)
            sync.dma_start(out=mm, in_=fin[:, 4:6]).then_inc(s_st, 16)
            c_st.add(16)
            sync.wait_ge(s_st, c_st.v)
    return nc



def _emit_scalar_broadcast(nc, block, ln, sems, ctrs, srcrow, dstcols, name):
    (s_pe, s_se, s_ve) = sems
    (c_pe, c_se, c_ve) = ctrs
    n = srcrow.shape[1]
    with nc.psum_tensor(name, [NP, n], F32) as psB:
        @block.tensor
        def _(te):
            te.matmul(psB[:], ln.ones_1p[:], srcrow, start=True, stop=True).then_inc(s_pe, 1)
            c_pe.add(1)
        pe_b = c_pe.v

        @block.vector
        def _(ve):
            ve.wait_ge(s_pe, pe_b)
            ve.tensor_copy(dstcols, psB[:]).then_inc(s_ve, 1)
            c_ve.add(1)
    return c_ve.v


def build_l1_q34(stage):
    nc = bass.Bass("TRN2", target_bir_lowering=False)
    gtr = nc.dram_tensor("gtr", [3 * D, TL], F32, kind="ExternalInput").ap()
    mg = nc.dram_tensor("mg", [1, 8], F32, kind="ExternalInput").ap()
    if stage == 4:
        upr = nc.dram_tensor("upr", [3 * D, TL], F32, kind="ExternalInput").ap()
        mu = nc.dram_tensor("mu", [1, 8], F32, kind="ExternalInput").ap()
        mp1 = nc.dram_tensor("mp1", [1, 8], F32, kind="ExternalInput").ap()
    mp = nc.dram_tensor("mp", [1, 1], F32, kind="ExternalOutput").ap()
    NI = 3 * D // NP

    with ExitStack() as ctx:
        E = ctx.enter_context
        gb = E(nc.sbuf_tensor("gb", [NP, 2 * TL], F32))
        ub = E(nc.sbuf_tensor("ub", [NP, 2 * TL], F32)) if stage == 4 else None
        tA = E(nc.sbuf_tensor("tA", [NP, TL], F32))
        tB = E(nc.sbuf_tensor("tB", [NP, TL], F32))
        tC = E(nc.sbuf_tensor("tC", [NP, TL], F32))
        tD = E(nc.sbuf_tensor("tD", [NP, TL], F32))
        tI = E(nc.sbuf_tensor("tI", [NP, TL], I32))
        macc = E(nc.sbuf_tensor("macc", [NP, 4], F32))
        id128 = E(nc.sbuf_tensor("id128", [NP, NP], F32))
        id128i = E(nc.sbuf_tensor("id128i", [NP, NP], I32))
        fin = E(nc.sbuf_tensor("fin", [1, 32], F32))
        sc = E(nc.sbuf_tensor("sc", [NP, 8], F32))
        sc2 = E(nc.sbuf_tensor("sc2", [NP, 2], F32))
        ones_1p = E(nc.sbuf_tensor("o1p", [1, NP], F32))
        s_l0 = E(nc.semaphore("s_l0"))
        s_l1 = E(nc.semaphore("s_l1"))
        s_lm = E(nc.semaphore("s_lm"))
        s_pe = E(nc.semaphore("s_pe"))
        s_se = E(nc.semaphore("s_se"))
        s_ve = E(nc.semaphore("s_ve"))
        s_gp = E(nc.semaphore("s_gp"))
        s_st = E(nc.semaphore("s_st"))
        block = E(nc.Block())
        c_pe, c_se, c_ve, c_gp, c_st, c_lm = (Ctr() for _ in range(6))
        c_l = [Ctr(), Ctr()]
        lsem = [s_l0, s_l1]

        class LnShim:
            pass
        ln = LnShim()
        ln.ones_1p = ones_1p

        @block.sync
        def _(sync):
            sync.dma_start(out=fin[:, 0:8], in_=mg).then_inc(s_lm, 16)
            c_lm.add(16)
            if stage == 4:
                sync.dma_start(out=fin[:, 8:16], in_=mu).then_inc(s_lm, 16)
                sync.dma_start(out=fin[:, 16:24], in_=mp1).then_inc(s_lm, 16)
                c_lm.add(32)
        ld_m = c_lm.v

        @block.gpsimd
        def _(g):
            g.iota(id128i[:], pattern=[[-1, NP]], base=0, channel_multiplier=1).then_inc(s_gp, 1)
            c_gp.add(1)

        @block.vector
        def _(ve):
            def ss(i):
                i.then_inc(s_ve, 1)
                ve.wait_ge(s_ve, c_ve.add(1))
            ve.wait_ge(s_gp, c_gp.v)
            ve.wait_ge(s_lm, ld_m)
            ss(ve.memset(ones_1p[:], 1.0))
            ss(ve.memset(macc[:], 0.0))
            ss(ve.memset(fin[:, 24:32], 0.0))
            ss(ve.tensor_copy(id128[:], id128i[:]))
            ss(ve.tensor_single_scalar(id128[:], id128[:], 0.0, ALU.is_equal))
            ss(ve.reduce_max(fin[:, 24:25], fin[:, 0:8], axis=AX.X))
            ss(ve.tensor_scalar(fin[:, 25:26], fin[:, 24:25], 1.0 / 32767.0, 1e-12,
                                ALU.mult, ALU.max))
            ss(ve.reciprocal(fin[:, 26:27], fin[:, 25:26]))
            if stage == 4:
                ss(ve.reduce_max(fin[:, 27:28], fin[:, 8:16], axis=AX.X))
                ss(ve.tensor_scalar(fin[:, 28:29], fin[:, 27:28], 1.0 / 32767.0, 1e-12,
                                    ALU.mult, ALU.max))
                ss(ve.reciprocal(fin[:, 29:30], fin[:, 28:29]))
                ss(ve.reduce_max(fin[:, 30:31], fin[:, 16:24], axis=AX.X))
                ss(ve.tensor_scalar(fin[:, 30:31], fin[:, 30:31], 1.0 / 65536.0, None, ALU.mult))
                ss(ve.tensor_scalar(fin[:, 30:31], fin[:, 30:31], 1.0 / 32767.0, 1e-12,
                                    ALU.mult, ALU.max))
                ss(ve.tensor_scalar(fin[:, 31:32], fin[:, 30:31], 65536.0, None, ALU.mult))
                ss(ve.reciprocal(fin[:, 31:32], fin[:, 31:32]))
        ve_sc = c_ve.v

        with nc.psum_tensor("psB34", [NP, 8], F32) as psB:
            @block.tensor
            def _(te):
                te.wait_ge(s_ve, ve_sc)
                te.matmul(psB[:], ones_1p[:], fin[:, 24:32], start=True, stop=True).then_inc(s_pe, 1)
                c_pe.add(1)
            pe_b = c_pe.v

            @block.vector
            def _(ve):
                def ss(i):
                    i.then_inc(s_ve, 1)
                    ve.wait_ge(s_ve, c_ve.add(1))
                ve.wait_ge(s_pe, pe_b)
                ss(ve.tensor_copy(sc[:, 0:1], psB[:, 1:2]))
                ss(ve.tensor_scalar(sc[:, 1:2], psB[:, 1:2], -1.0, None, ALU.mult))
                ss(ve.tensor_copy(sc[:, 2:3], psB[:, 2:3]))
                if stage == 4:
                    ss(ve.tensor_copy(sc[:, 6:7], psB[:, 4:5]))
                    ss(ve.tensor_scalar(sc[:, 7:8], psB[:, 4:5], -1.0, None, ALU.mult))
                    ss(ve.tensor_copy(sc2[:, 0:1], psB[:, 5:6]))
                    ss(ve.tensor_copy(sc[:, 3:4], psB[:, 6:7]))
                    ss(ve.tensor_scalar(sc[:, 4:5], psB[:, 6:7], -1.0, None, ALU.mult))
                    ss(ve.tensor_copy(sc[:, 5:6], psB[:, 7:8]))
        ve_bc = c_ve.v

        ve_done = {}
        for ic in range(NI):
            slot = ic % 2
            war = ve_done.get(ic - 2)

            @block.sync
            def _(sync, _ic=ic, _slot=slot, _war=war):
                if _war is not None:
                    sync.wait_ge(s_ve, _war)
                sync.dma_start(out=gb[:, TL * _slot:TL * (_slot + 1)],
                               in_=gtr[NP * _ic:NP * (_ic + 1), :]).then_inc(lsem[_slot], 16)
                c_l[_slot].add(16)
                if stage == 4:
                    sync.dma_start(out=ub[:, TL * _slot:TL * (_slot + 1)],
                                   in_=upr[NP * _ic:NP * (_ic + 1), :]).then_inc(lsem[_slot], 16)
                    c_l[_slot].add(16)
            ld_c = (lsem[slot], c_l[slot].v)

            @block.vector
            def _(ve, _slot=slot, _ld=ld_c, _ic=ic):
                def ss(i):
                    i.then_inc(s_ve, 1)
                    ve.wait_ge(s_ve, c_ve.add(1))
                ve.wait_ge(_ld[0], _ld[1])
                if _ic == 0:
                    ve.wait_ge(s_ve, ve_bc)
                gv = gb[:, TL * _slot:TL * (_slot + 1)]
                ss(ve.tensor_scalar(tA[:], gv, sc[:, 2:3], None, ALU.mult))
                emit_round_chain(ve, ss, tA[:], tI[:], tB[:], tC[:])
                ss(ve.tensor_scalar(tA[:], tB[:], sc[:, 0:1], sc[:, 1:2], ALU.mult, ALU.add))
            ve_gq = c_ve.v

            @block.scalar
            def _(se, _ve=ve_gq):
                se.wait_ge(s_ve, _ve)
                se.activation(tB[:], tA[:], AF.Sigmoid).then_inc(s_se, 1)
                c_se.add(1)
            se_sg = c_se.v

            @block.vector
            def _(ve, _slot=slot, _se=se_sg):
                def ss(i):
                    i.then_inc(s_ve, 1)
                    ve.wait_ge(s_ve, c_ve.add(1))
                ve.wait_ge(s_se, _se)
                ss(ve.tensor_scalar(tC[:], tB[:], 65536.0, None, ALU.mult))
                emit_round_chain(ve, ss, tC[:], tI[:], tB[:], tD[:])
                ss(ve.tensor_scalar(tB[:], tB[:], 65536.0, None, ALU.min))
                ss(ve.scalar_tensor_tensor(tB[:], tB[:], -1.0, tA[:], ALU.add, ALU.mult))
                if stage == 3:
                    ss(ve.reduce_max(macc[:, 3:4], tB[:], axis=AX.X, apply_absolute_value=True))
                    ss(ve.tensor_tensor(macc[:, 0:1], macc[:, 0:1], macc[:, 3:4], ALU.max))
                else:
                    ss(ve.tensor_scalar(tA[:], tB[:], sc[:, 5:6], None, ALU.mult))
                    emit_round_chain(ve, ss, tA[:], tI[:], tB[:], tC[:])
                    ss(ve.tensor_scalar(tA[:], tB[:], sc[:, 3:4], sc[:, 4:5], ALU.mult, ALU.add))
                    uv = ub[:, TL * _slot:TL * (_slot + 1)]
                    ss(ve.tensor_scalar(tB[:], uv, sc2[:, 0:1], None, ALU.mult))
                    emit_round_chain(ve, ss, tB[:], tI[:], tC[:], tD[:])
                    ss(ve.tensor_scalar(tB[:], tC[:], sc[:, 6:7], sc[:, 7:8], ALU.mult, ALU.add))
                    ss(ve.tensor_tensor(tB[:], tA[:], tB[:], ALU.mult))
                    ss(ve.reduce_max(macc[:, 3:4], tB[:], axis=AX.X, apply_absolute_value=True))
                    ss(ve.tensor_tensor(macc[:, 0:1], macc[:, 0:1], macc[:, 3:4], ALU.max))
            ve_done[ic] = c_ve.v

        ve_all = c_ve.v
        with nc.psum_tensor("psT3", [1, NP], F32) as psT:
            @block.tensor
            def _(te):
                te.wait_ge(s_ve, ve_all)
                te.transpose(psT[:], macc[:, 0:1], id128[:]).then_inc(s_pe, 1)
                c_pe.add(1)
            pe_t = c_pe.v

            @block.vector
            def _(ve):
                ve.wait_ge(s_pe, pe_t)
                ve.reduce_max(fin[:, 0:1], psT[:], axis=AX.X).then_inc(s_ve, 1)
                c_ve.add(1)
            ve_f = c_ve.v

        @block.sync
        def _(sync):
            sync.wait_ge(s_ve, ve_f)
            sync.dma_start(out=mp, in_=fin[:, 0:1]).then_inc(s_st, 16)
            c_st.add(16)
            sync.wait_ge(s_st, c_st.v)
    return nc



def build_l1_q5():
    nc = bass.Bass("TRN2", target_bir_lowering=False)
    x3 = nc.dram_tensor("x3", [D, TL], F32, kind="ExternalInput").ap()
    gtr = nc.dram_tensor("gtr", [3 * D, TL], F32, kind="ExternalInput").ap()
    upr = nc.dram_tensor("upr", [3 * D, TL], F32, kind="ExternalInput").ap()
    mg = nc.dram_tensor("mg", [1, 8], F32, kind="ExternalInput").ap()
    mu = nc.dram_tensor("mu", [1, 8], F32, kind="ExternalInput").ap()
    mp1 = nc.dram_tensor("mp1", [1, 8], F32, kind="ExternalInput").ap()
    mp2 = nc.dram_tensor("mp2", [1, 8], F32, kind="ExternalInput").ap()
    wdn = nc.dram_tensor("wdn", [D, 3 * D], F32, kind="ExternalInput").ap()
    x4 = nc.dram_tensor("x4", [D, TL], F32, kind="ExternalOutput").ap()
    wqd_dram = nc.dram_tensor("wqd_dram", [D, 3 * D], BF16).ap()
    NI = 3 * D // NP

    with ExitStack() as ctx:
        E = ctx.enter_context
        gb = E(nc.sbuf_tensor("gb", [NP, 2 * 512], F32))
        ub = E(nc.sbuf_tensor("ub", [NP, 2 * 512], F32))
        x3b = E(nc.sbuf_tensor("x3b", [NP, 8 * 512], F32))
        tA = E(nc.sbuf_tensor("tA", [NP, 3 * D], F32))
        tB = E(nc.sbuf_tensor("tB", [NP, 3 * D], F32))
        tC = E(nc.sbuf_tensor("tC", [NP, 3 * D], F32))
        tD = E(nc.sbuf_tensor("tD", [NP, 3 * D], F32))
        tI = E(nc.sbuf_tensor("tI", [NP, 3 * D], I32))
        wbuf = E(nc.sbuf_tensor("wbuf", [NP, 2 * 3 * D], F32))
        wsc = E(nc.sbuf_tensor("wsc", [NP, 3 * 96], F32))
        wdT = E(nc.sbuf_tensor("wdT", [NP, NI * D], BF16))
        wqb5 = E(nc.sbuf_tensor("wqb5", [NP, 2 * 3 * D], BF16))
        oq = E(nc.sbuf_tensor("oq", [NP, NI * 512], BF16))
        ev = E(nc.sbuf_tensor("ev", [NP, 2 * 512], F32))
        fin = E(nc.sbuf_tensor("fin", [1, 48], F32))
        sc = E(nc.sbuf_tensor("sc", [NP, 10], F32))
        ones_1p = E(nc.sbuf_tensor("o1p", [1, NP], F32))
        s_lm = E(nc.semaphore("s_lm"))
        s_lw0 = E(nc.semaphore("s_lw0"))
        s_lw1 = E(nc.semaphore("s_lw1"))
        s_wq = E(nc.semaphore("s_wq"))
        s_tq = E(nc.semaphore("s_tq"))
        s_l0 = E(nc.semaphore("s_l0"))
        s_l1 = E(nc.semaphore("s_l1"))
        s_lx = E(nc.semaphore("s_lx"))
        s_pe = E(nc.semaphore("s_pe"))
        s_se = E(nc.semaphore("s_se"))
        s_ve = E(nc.semaphore("s_ve"))
        s_st0 = E(nc.semaphore("s_st0"))
        s_st1 = E(nc.semaphore("s_st1"))
        block = E(nc.Block())
        c_lm, c_wq, c_tq, c_lx, c_pe, c_se, c_ve = (Ctr() for _ in range(7))
        c_lw = [Ctr(), Ctr()]
        c_l = [Ctr(), Ctr()]
        c_sto = [Ctr(), Ctr()]
        lwsem = [s_lw0, s_lw1]
        lsem = [s_l0, s_l1]
        stsem = [s_st0, s_st1]

        @block.sync
        def _(sync):
            for (o, ap) in [(0, mg), (8, mu), (16, mp1), (24, mp2)]:
                sync.dma_start(out=fin[:, o:o + 8], in_=ap).then_inc(s_lm, 16)
                c_lm.add(16)
        ld_m = c_lm.v

        sc4 = E(nc.sbuf_tensor("sc4", [NP, 3], F32))

        @block.vector
        def _(ve):
            def ss(i):
                i.then_inc(s_ve, 1)
                ve.wait_ge(s_ve, c_ve.add(1))
            ve.wait_ge(s_lm, ld_m)
            ss(ve.memset(ones_1p[:], 1.0))
            ss(ve.reduce_max(fin[:, 40:41], fin[:, 0:8], axis=AX.X))
            ss(ve.tensor_scalar(fin[:, 32:33], fin[:, 40:41], 1.0 / 32767.0, 1e-12,
                                ALU.mult, ALU.max))
            ss(ve.reciprocal(fin[:, 33:34], fin[:, 32:33]))
            ss(ve.reduce_max(fin[:, 40:41], fin[:, 8:16], axis=AX.X))
            ss(ve.tensor_scalar(fin[:, 34:35], fin[:, 40:41], 1.0 / 32767.0, 1e-12,
                                ALU.mult, ALU.max))
            ss(ve.reciprocal(fin[:, 35:36], fin[:, 34:35]))
            ss(ve.reduce_max(fin[:, 40:41], fin[:, 16:24], axis=AX.X))
            ss(ve.tensor_scalar(fin[:, 40:41], fin[:, 40:41], 1.0 / 65536.0, None, ALU.mult))
            ss(ve.tensor_scalar(fin[:, 36:37], fin[:, 40:41], 1.0 / 32767.0, 1e-12,
                                ALU.mult, ALU.max))
            ss(ve.tensor_scalar(fin[:, 37:38], fin[:, 36:37], 65536.0, None, ALU.mult))
            ss(ve.reciprocal(fin[:, 37:38], fin[:, 37:38]))
            ss(ve.reduce_max(fin[:, 40:41], fin[:, 24:32], axis=AX.X))
            ss(ve.tensor_scalar(fin[:, 38:39], fin[:, 40:41], 1.0 / 32767.0, 1e-12,
                                ALU.mult, ALU.max))
            ss(ve.reciprocal(fin[:, 39:40], fin[:, 38:39]))
        ve_sc = c_ve.v

        with nc.psum_tensor("psB5", [NP, 8], F32) as psB:
            @block.tensor
            def _(te):
                te.wait_ge(s_ve, ve_sc)
                te.matmul(psB[:], ones_1p[:], fin[:, 32:40], start=True, stop=True).then_inc(s_pe, 1)
                c_pe.add(1)
            pe_b = c_pe.v

            @block.vector
            def _(ve):
                def ss(i):
                    i.then_inc(s_ve, 1)
                    ve.wait_ge(s_ve, c_ve.add(1))
                ve.wait_ge(s_pe, pe_b)
                ss(ve.tensor_copy(sc[:, 0:1], psB[:, 0:1]))
                ss(ve.tensor_scalar(sc[:, 1:2], psB[:, 0:1], -1.0, None, ALU.mult))
                ss(ve.tensor_copy(sc[:, 2:3], psB[:, 1:2]))
                ss(ve.tensor_copy(sc[:, 6:7], psB[:, 2:3]))
                ss(ve.tensor_scalar(sc[:, 7:8], psB[:, 2:3], -1.0, None, ALU.mult))
                ss(ve.tensor_copy(sc[:, 8:9], psB[:, 3:4]))
                ss(ve.tensor_copy(sc[:, 3:4], psB[:, 4:5]))
                ss(ve.tensor_scalar(sc[:, 4:5], psB[:, 4:5], -1.0, None, ALU.mult))
                ss(ve.tensor_copy(sc[:, 5:6], psB[:, 5:6]))
                ss(ve.tensor_copy(sc4[:, 0:1], psB[:, 6:7]))
                ss(ve.tensor_scalar(sc4[:, 1:2], psB[:, 6:7], -1.0, None, ALU.mult))
                ss(ve.tensor_copy(sc4[:, 2:3], psB[:, 7:8]))
        ve_bc = c_ve.v

        ve_dq_at = {}
        for r in range(8):
            slot = r % 2
            war = ve_dq_at.get(r - 2)

            @block.sync
            def _(sync, _r=r, _slot=slot, _war=war):
                if _war is not None:
                    sync.wait_ge(s_ve, _war)
                sync.dma_start(out=wbuf[:, 3 * D * _slot:3 * D * (_slot + 1)],
                               in_=wdn[NP * _r:NP * (_r + 1), :]).then_inc(lwsem[_slot], 16)
                c_lw[_slot].add(16)
            ld_w = (lwsem[slot], c_lw[slot].v)

            @block.vector
            def _(ve, _r=r, _slot=slot, _ld=ld_w, _wq=(c_wq.v if r >= 2 else None)):
                def ss(i):
                    i.then_inc(s_ve, 1)
                    ve.wait_ge(s_ve, c_ve.add(1))
                ve.wait_ge(_ld[0], _ld[1])
                if _wq is not None:
                    ve.wait_ge(s_wq, _wq)
                if _r == 0:
                    ve.wait_ge(s_ve, ve_bc)
                wv = wbuf[:, 3 * D * _slot:3 * D * (_slot + 1)]
                w3 = wv.rearrange("p (a b) -> p a b", b=32)
                mx = wsc[:, 0:96]; sc_ = wsc[:, 96:192]; rc = wsc[:, 192:288]
                ss(ve.reduce_max(mx, w3, axis=AX.X, apply_absolute_value=True))
                ss(ve.tensor_scalar(sc_, mx, 1.0 / 7.0, 1e-12, ALU.mult, ALU.max))
                ss(ve.reciprocal(rc, sc_))
                rc3 = rc.unsqueeze(2).broadcast_to([NP, 96, 32])
                sc3 = sc_.unsqueeze(2).broadcast_to([NP, 96, 32])
                y3 = tA[:].rearrange("p (a b) -> p a b", b=32)
                ss(ve.tensor_tensor(y3, w3, rc3, ALU.mult))
                emit_round_chain(ve, ss, tA[:], tI[:], tB[:], tC[:])
                r3 = tB[:].rearrange("p (a b) -> p a b", b=32)
                u3 = tC[:].rearrange("p (a b) -> p a b", b=32)
                ss(ve.tensor_tensor(u3, r3, sc3, ALU.mult))
                q3 = wqb5[:, 3 * D * _slot:3 * D * (_slot + 1)].rearrange("p (a b) -> p a b", b=32)
                ss(ve.tensor_tensor(q3, u3, sc3, ALU.subtract))
            ve_dq_at[r] = c_ve.v

            @block.sync
            def _(sync, _r=r, _slot=slot, _ve=ve_dq_at[r]):
                sync.wait_ge(s_ve, _ve)
                sync.dma_start(out=wqd_dram[NP * _r:NP * (_r + 1), :],
                               in_=wqb5[:, 3 * D * _slot:3 * D * (_slot + 1)]).then_inc(s_wq, 16)
                c_wq.add(16)
        wq_done = c_wq.v

        @block.sync
        def _(sync):
            sync.wait_ge(s_wq, wq_done)
            for ic in range(NI):
                sync.dma_start(out=wdT[:, D * ic:D * (ic + 1)],
                               in_=wqd_dram[:, NP * ic:NP * (ic + 1)],
                               transpose=True).then_inc(s_tq, 16)
                c_tq.add(16)
        tq_done = c_tq.v


        ve_oq_prev = 0
        st_prev5 = {}
        ve_d_at = {}
        psD_ctx = nc.psum_tensor("psD", [NP, 2 * 512], F32)
        psD = psD_ctx.__enter__()
        for tt in range(4):
            @block.sync
            def _(sync, _tt=tt, _vp=ve_oq_prev):
                if _vp:
                    sync.wait_ge(s_ve, _vp)
                for c in range(8):
                    sync.dma_start(out=x3b[:, 512 * c:512 * (c + 1)],
                                   in_=x3[NP * c:NP * (c + 1), 512 * _tt:512 * (_tt + 1)]).then_inc(
                                       s_lx, 16)
                    c_lx.add(16)
            ld_x = c_lx.v

            ve_oq_at = {}
            for ic in range(NI):
                slot = ic % 2
                war = ve_oq_at.get(ic - 2, ve_oq_prev if ic < 2 else None)

                @block.sync
                def _(sync, _ic=ic, _tt=tt, _slot=slot, _war=war):
                    if _war is not None:
                        sync.wait_ge(s_ve, _war)
                    sync.dma_start(out=gb[:, 512 * _slot:512 * (_slot + 1)],
                                   in_=gtr[NP * _ic:NP * (_ic + 1), 512 * _tt:512 * (_tt + 1)]).then_inc(
                                       lsem[_slot], 16)
                    sync.dma_start(out=ub[:, 512 * _slot:512 * (_slot + 1)],
                                   in_=upr[NP * _ic:NP * (_ic + 1), 512 * _tt:512 * (_tt + 1)]).then_inc(
                                       lsem[_slot], 16)
                    c_l[_slot].add(32)
                ld_c = (lsem[slot], c_l[slot].v)

                @block.vector
                def _(ve, _ic=ic, _slot=slot, _ld=ld_c, _tt=tt):
                    def ss(i):
                        i.then_inc(s_ve, 1)
                        ve.wait_ge(s_ve, c_ve.add(1))
                    ve.wait_ge(_ld[0], _ld[1])
                    if _tt == 0 and _ic == 0:
                        ve.wait_ge(s_tq, tq_done)
                    A = tA[:, 0:512]; B = tB[:, 0:512]; C = tC[:, 0:512]; Dd = tD[:, 0:512]
                    Ii = tI[:, 0:512]
                    gv = gb[:, 512 * _slot:512 * (_slot + 1)]
                    uv = ub[:, 512 * _slot:512 * (_slot + 1)]
                    ss(ve.tensor_scalar(A, gv, sc[:, 2:3], None, ALU.mult))
                    emit_round_chain(ve, ss, A, Ii, B, C)
                    ss(ve.tensor_scalar(A, B, sc[:, 0:1], sc[:, 1:2], ALU.mult, ALU.add))
                ve_gq = c_ve.v

                @block.scalar
                def _(se, _ve=ve_gq):
                    se.wait_ge(s_ve, _ve)
                    se.activation(tB[:, 0:512], tA[:, 0:512], AF.Sigmoid).then_inc(s_se, 1)
                    c_se.add(1)
                se_sg = c_se.v

                @block.vector
                def _(ve, _ic=ic, _slot=slot, _se=se_sg):
                    def ss(i):
                        i.then_inc(s_ve, 1)
                        ve.wait_ge(s_ve, c_ve.add(1))
                    ve.wait_ge(s_se, _se)
                    A = tA[:, 0:512]; B = tB[:, 0:512]; C = tC[:, 0:512]; Dd = tD[:, 0:512]
                    Ii = tI[:, 0:512]
                    uv = ub[:, 512 * _slot:512 * (_slot + 1)]
                    ss(ve.tensor_scalar(C, B, 65536.0, None, ALU.mult))
                    emit_round_chain(ve, ss, C, Ii, B, Dd)
                    ss(ve.tensor_scalar(B, B, 65536.0, None, ALU.min))
                    ss(ve.scalar_tensor_tensor(B, B, -1.0, A, ALU.add, ALU.mult))
                    ss(ve.tensor_scalar(A, B, sc[:, 5:6], None, ALU.mult))
                    emit_round_chain(ve, ss, A, Ii, B, C)
                    ss(ve.tensor_scalar(A, B, sc[:, 3:4], sc[:, 4:5], ALU.mult, ALU.add))
                    ss(ve.tensor_scalar(B, uv, sc[:, 8:9], None, ALU.mult))
                    emit_round_chain(ve, ss, B, Ii, C, Dd)
                    ss(ve.tensor_scalar(B, C, sc[:, 6:7], sc[:, 7:8], ALU.mult, ALU.add))
                    ss(ve.tensor_tensor(B, A, B, ALU.mult))
                    ss(ve.tensor_scalar(A, B, sc4[:, 2:3], None, ALU.mult))
                    emit_round_chain(ve, ss, A, Ii, B, C)
                    ss(ve.tensor_scalar(oq[:, 512 * _ic:512 * (_ic + 1)], B,
                                        sc4[:, 0:1], sc4[:, 1:2], ALU.mult, ALU.add))
                ve_oq_at[_ic if False else ic] = c_ve.v
            ve_oq_prev = c_ve.v

            for dc in range(8):
                job5 = tt * 8 + dc
                par = job5 % 2

                @block.tensor
                def _(te, _dc=dc, _tt=tt, _par=par, _job=job5, _vq=ve_oq_prev):
                    if _dc == 0:
                        te.wait_ge(s_ve, _vq)
                        te.wait_ge(s_lx, ld_x)
                    if _job >= 2:
                        te.wait_ge(s_ve, ve_d_at[_job - 2])
                    for ic in range(NI):
                        i = te.matmul(psD[:, 512 * _par:512 * (_par + 1)],
                                      wdT[:, D * ic + NP * _dc:D * ic + NP * (_dc + 1)],
                                      oq[:, 512 * ic:512 * (ic + 1)],
                                      start=(ic == 0), stop=(ic == NI - 1))
                    i.then_inc(s_pe, 1)
                    c_pe.add(1)
                pe_d = c_pe.v

                @block.vector
                def _(ve, _dc=dc, _par=par, _pe=pe_d, _war=st_prev5.get(par)):
                    ve.wait_ge(s_pe, _pe)
                    if _war is not None:
                        ve.wait_ge(stsem[_par], _war)
                    ve.tensor_tensor(ev[:, 512 * _par:512 * (_par + 1)],
                                     x3b[:, 512 * _dc:512 * (_dc + 1)],
                                     psD[:, 512 * _par:512 * (_par + 1)],
                                     ALU.add).then_inc(s_ve, 1)
                    c_ve.add(1)
                ve_d_at[job5] = c_ve.v

                @block.sync
                def _(sync, _dc=dc, _tt=tt, _par=par, _ve=ve_d_at[job5]):
                    sync.wait_ge(s_ve, _ve)
                    sync.dma_start(out=x4[NP * _dc:NP * (_dc + 1), 512 * _tt:512 * (_tt + 1)],
                                   in_=ev[:, 512 * _par:512 * (_par + 1)]).then_inc(
                                       stsem[_par], 16)
                    c_sto[_par].add(16)
                st_prev5[par] = c_sto[par].v

        psD_ctx.__exit__(None, None, None)

        @block.sync
        def _(sync):
            sync.wait_ge(s_st0, c_sto[0].v)
            sync.wait_ge(s_st1, c_sto[1].v)
    return nc



def _run(nc, in_maps, core_ids):
    return run_bass_kernel_spmd(nc, in_maps, core_ids=core_ids)


def kernel(L0_attn_x, L0_attn_Wc_down, L0_attn_Wq_up, L0_attn_Wkv_up, L0_attn_Wc_proj,
           L0_ffn_alpha, L0_ffn_w_gate, L0_ffn_b_gate, L0_ffn_W1m, L0_ffn_b1m,
           L0_ffn_W2m, L0_ffn_b2m, L0_ffn_W1e, L0_ffn_b1e, L0_ffn_W2e, L0_ffn_b2e,
           L1_attn_phazor_init_re, L1_attn_phazor_init_im, L1_attn_phazor_angle,
           L1_attn_ln_minus_ln_phazor_abs, L1_attn_last_conv_re, L1_attn_last_conv_im,
           L1_attn_ln_phazor_scale, L1_ffn_w_gate, L1_ffn_w_up, L1_ffn_w_down):
    f32 = np.float32
    x = np.ascontiguousarray(np.asarray(L0_attn_x, f32))
    B4 = x.shape[0]
    xT = [np.ascontiguousarray(x[b].T) for b in range(B4)]

    wA = {"wcd": np.asarray(L0_attn_Wc_down, f32), "wqu": np.asarray(L0_attn_Wq_up, f32),
          "wkv": np.asarray(L0_attn_Wkv_up, f32), "wcp": np.asarray(L0_attn_Wc_proj, f32)}

    ncA0 = build_l0_attn(0)
    rA0 = _run(ncA0, [dict(xT=np.ascontiguousarray(xT[b][:, :TL]), **wA) for b in range(B4)],
               [0, 1, 2, 3]).results
    ncA1 = build_l0_attn(1)
    rA1 = _run(ncA1, [dict(xT=xT[b], **wA) for b in range(B4)], [0, 1, 2, 3]).results
    x1 = {(b, 0): np.asarray(rA0[b]["x1"]) for b in range(B4)}
    x1.update({(b, 1): np.asarray(rA1[b]["x1"]) for b in range(B4)})

    ncG = build_l0_gate()
    rG = _run(ncG, [{"x1a": x1[(b, 0)], "x1b": x1[(b, 1)],
                     "wg": np.asarray(L0_ffn_w_gate, f32),
                     "bg": np.asarray(L0_ffn_b_gate, f32).reshape(1, 8)}
                    for b in range(B4)], [0, 1, 2, 3]).results
    gouts = [np.asarray(rG[b]["gout"]).reshape(8).astype(f32) for b in range(B4)]

    W1e = np.asarray(L0_ffn_W1e, f32); b1e = np.asarray(L0_ffn_b1e, f32)
    W2e = np.asarray(L0_ffn_W2e, f32); b2e = np.asarray(L0_ffn_b2e, f32)
    ncF = build_l0_ffn()
    maps = []
    for b in range(B4):
        i1, i2 = int(gouts[b][0]), int(gouts[b][1])
        for h in range(2):
            maps.append({"x1": x1[(b, h)], "gates": gouts[b].reshape(1, 8),
                         "alpha": np.asarray(L0_ffn_alpha, f32).reshape(1, 1),
                         "w1m": np.asarray(L0_ffn_W1m, f32),
                         "b1m": np.asarray(L0_ffn_b1m, f32).reshape(1, -1),
                         "w2m": np.asarray(L0_ffn_W2m, f32),
                         "b2m": np.asarray(L0_ffn_b2m, f32).reshape(1, -1),
                         "w1g": np.ascontiguousarray(W1e[[i1, i2]]),
                         "b1g": np.ascontiguousarray(b1e[[i1, i2]]),
                         "w2g": np.ascontiguousarray(W2e[[i1, i2]]),
                         "b2g": np.ascontiguousarray(b2e[[i1, i2]])})
    rF = _run(ncF, maps, list(range(8))).results
    x2 = {(b, h): np.asarray(rF[2 * b + h]["x2"]) for b in range(B4) for h in range(2)}

    x2f = [np.ascontiguousarray(np.concatenate([x2[(b, 0)], x2[(b, 1)]], axis=1))
           for b in range(B4)]

    def prm(a, g):
        return np.ascontiguousarray(np.asarray(a, f32)[512 * g:512 * (g + 1)].reshape(4, NP).T)

    ncM = build_l1_attn()
    maps = []
    for b in range(B4):
        for g in range(2):
            rows = np.concatenate([x2f[b][512 * g:512 * (g + 1)],
                                   x2f[b][512 * (1 - g):512 * (2 - g)]], axis=0)
            maps.append({"x2f": np.ascontiguousarray(rows),
                         "p_ire": prm(L1_attn_phazor_init_re, g),
                         "p_iim": prm(L1_attn_phazor_init_im, g),
                         "p_ang": prm(L1_attn_phazor_angle, g),
                         "p_lml": prm(L1_attn_ln_minus_ln_phazor_abs, g),
                         "p_lcr": prm(L1_attn_last_conv_re, g),
                         "p_lci": prm(L1_attn_last_conv_im, g),
                         "p_lsc": prm(L1_attn_ln_phazor_scale, g)})
    rM = _run(ncM, maps, list(range(8))).results
    x3f = []
    for b in range(B4):
        xb3 = np.empty((D, T), f32)
        for g in range(2):
            xb3[512 * g:512 * (g + 1)] = np.asarray(rM[2 * b + g]["x3p"])
        x3f.append(xb3)
    x3 = {(b, h): np.ascontiguousarray(x3f[b][:, TL * h:TL * (h + 1)])
          for b in range(B4) for h in range(2)}

    order = [(b, h) for b in range(B4) for h in range(2)]
    ncQ1 = build_l1_q1()
    r1 = _run(ncQ1, [{"x3": x3[k]} for k in order], list(range(8))).results
    m1g = np.array([[float(np.asarray(r1[i]["m1"])[0, 0]) for i in range(8)]], f32)

    ncQ2 = build_l1_q2()
    r2 = _run(ncQ2, [{"x3": x3[k], "m1g": m1g,
                      "wup": np.asarray(L1_ffn_w_up, f32),
                      "wgt": np.asarray(L1_ffn_w_gate, f32)} for k in order],
              list(range(8))).results
    mu_g = np.array([[float(np.asarray(r2[i]["mm"])[0, 0]) for i in range(8)]], f32)
    mg_g = np.array([[float(np.asarray(r2[i]["mm"])[0, 1]) for i in range(8)]], f32)
    upr = {order[i]: np.asarray(r2[i]["upr"]) for i in range(8)}
    gtr = {order[i]: np.asarray(r2[i]["gtr"]) for i in range(8)}

    ncQ3 = build_l1_q3b()
    r3 = _run(ncQ3, [{"gtr": gtr[k], "mg": mg_g} for k in order], list(range(8))).results
    mp1_g = np.array([[float(np.asarray(r3[i]["mp"])[0, 0]) for i in range(8)]], f32)
    kg = {order[i]: np.asarray(r3[i]["kg_o"]) for i in range(8)}
    ks = {order[i]: np.asarray(r3[i]["ks_o"]) for i in range(8)}

    ncQ4 = build_l1_q4b()
    r4 = _run(ncQ4, [{"upr": upr[k], "kg_i": kg[k], "ks_i": ks[k], "mg": mg_g,
                      "mu": mu_g, "mp1": mp1_g} for k in order], list(range(8))).results
    mp2_g = np.array([[float(np.asarray(r4[i]["mp"])[0, 0]) for i in range(8)]], f32)
    ku = {order[i]: np.asarray(r4[i]["ku_o"]) for i in range(8)}
    k3 = {order[i]: np.asarray(r4[i]["k3_o"]) for i in range(8)}

    ncQ5 = build_l1_q5b()
    r5 = _run(ncQ5, [{"x3": x3[k], "k3_i": k3[k], "ku_i": ku[k], "mg": mg_g,
                      "mu": mu_g, "mp1": mp1_g, "mp2": mp2_g,
                      "wdn": np.asarray(L1_ffn_w_down, f32)} for k in order],
              list(range(8))).results

    out = np.empty((B4, T, D), f32)
    for i, (b, h) in enumerate(order):
        out[b, TL * h:TL * (h + 1), :] = np.asarray(r5[i]["x4"]).T
    return out


I16 = mybir.dt.int16
U16 = mybir.dt.uint16


def emit_round4(ve, ss, y, ki, d):
    ss(ve.tensor_scalar(ki, y, 1.0, None, ALU.mult))
    ss(ve.scalar_tensor_tensor(d, ki, -1.0, y, ALU.mult, ALU.add))
    ss(ve.scalar_tensor_tensor(y, d, 0.5, ki, ALU.is_ge, ALU.add))
    ss(ve.scalar_tensor_tensor(y, d, -0.5, y, ALU.is_ge, ALU.add))


def build_l1_q3b():
    nc = bass.Bass("TRN2", target_bir_lowering=False)
    gtr = nc.dram_tensor("gtr", [3 * D, TL], F32, kind="ExternalInput").ap()
    mg = nc.dram_tensor("mg", [1, 8], F32, kind="ExternalInput").ap()
    mp = nc.dram_tensor("mp", [1, 1], F32, kind="ExternalOutput").ap()
    kg_o = nc.dram_tensor("kg_o", [3 * D, TL], I16, kind="ExternalOutput").ap()
    ks_o = nc.dram_tensor("ks_o", [3 * D, TL], U16, kind="ExternalOutput").ap()
    NI = 3 * D // NP

    with ExitStack() as ctx:
        E = ctx.enter_context
        gb = E(nc.sbuf_tensor("gb", [NP, 2 * TL], F32))
        tA = E(nc.sbuf_tensor("tA", [NP, TL], F32))
        tB = E(nc.sbuf_tensor("tB", [NP, TL], F32))
        tD = E(nc.sbuf_tensor("tD", [NP, TL], F32))
        tI = E(nc.sbuf_tensor("tI", [NP, TL], I32))
        kg_b = E(nc.sbuf_tensor("kg_b", [NP, 2 * TL], I16))
        ks_b = E(nc.sbuf_tensor("ks_b", [NP, 2 * TL], U16))
        macc = E(nc.sbuf_tensor("macc", [NP, 4], F32))
        id128 = E(nc.sbuf_tensor("id128", [NP, NP], F32))
        id128i = E(nc.sbuf_tensor("id128i", [NP, NP], I32))
        fin = E(nc.sbuf_tensor("fin", [1, 32], F32))
        sc = E(nc.sbuf_tensor("sc", [NP, 3], F32))
        ones_1p = E(nc.sbuf_tensor("o1p", [1, NP], F32))
        s_l0 = E(nc.semaphore("s_l0"))
        s_l1 = E(nc.semaphore("s_l1"))
        s_lm = E(nc.semaphore("s_lm"))
        s_pe = E(nc.semaphore("s_pe"))
        s_se = E(nc.semaphore("s_se"))
        s_ve = E(nc.semaphore("s_ve"))
        s_gp = E(nc.semaphore("s_gp"))
        s_k0 = E(nc.semaphore("s_k0"))
        s_k1 = E(nc.semaphore("s_k1"))
        s_st = E(nc.semaphore("s_st"))
        block = E(nc.Block())
        c_pe, c_se, c_ve, c_gp, c_st, c_lm = (Ctr() for _ in range(6))
        c_l = [Ctr(), Ctr()]
        c_k = [Ctr(), Ctr()]
        lsem = [s_l0, s_l1]
        ksem = [s_k0, s_k1]

        @block.sync
        def _(sync):
            sync.dma_start(out=fin[:, 0:8], in_=mg).then_inc(s_lm, 16)
            c_lm.add(16)
        ld_m = c_lm.v

        @block.gpsimd
        def _(g):
            g.iota(id128i[:], pattern=[[-1, NP]], base=0, channel_multiplier=1).then_inc(s_gp, 1)
            c_gp.add(1)

        @block.vector
        def _(ve):
            def ss(i):
                i.then_inc(s_ve, 1)
                ve.wait_ge(s_ve, c_ve.add(1))
            ve.wait_ge(s_gp, c_gp.v)
            ve.wait_ge(s_lm, ld_m)
            ss(ve.memset(ones_1p[:], 1.0))
            ss(ve.memset(macc[:], 0.0))
            ss(ve.tensor_copy(id128[:], id128i[:]))
            ss(ve.tensor_single_scalar(id128[:], id128[:], 0.0, ALU.is_equal))
            ss(ve.reduce_max(fin[:, 8:9], fin[:, 0:8], axis=AX.X))
            ss(ve.tensor_scalar(fin[:, 9:10], fin[:, 8:9], 1.0 / 32767.0, 1e-12,
                                ALU.mult, ALU.max))
            ss(ve.reciprocal(fin[:, 10:11], fin[:, 9:10]))
        ve_sc = c_ve.v

        with nc.psum_tensor("psB3b", [NP, 2], F32) as psB:
            @block.tensor
            def _(te):
                te.wait_ge(s_ve, ve_sc)
                te.matmul(psB[:], ones_1p[:], fin[:, 9:11], start=True, stop=True).then_inc(s_pe, 1)
                c_pe.add(1)
            pe_b = c_pe.v

            @block.vector
            def _(ve):
                def ss(i):
                    i.then_inc(s_ve, 1)
                    ve.wait_ge(s_ve, c_ve.add(1))
                ve.wait_ge(s_pe, pe_b)
                ss(ve.tensor_copy(sc[:, 0:1], psB[:, 0:1]))
                ss(ve.tensor_scalar(sc[:, 1:2], psB[:, 0:1], -1.0, None, ALU.mult))
                ss(ve.tensor_copy(sc[:, 2:3], psB[:, 1:2]))
        ve_bc = c_ve.v

        ve_done = {}
        st_done = {}
        sq3 = []
        for ic in range(NI):
            slot = ic % 2
            war = ve_done.get(ic - 2)

            @block.sync
            def _(sync, _ic=ic, _slot=slot, _war=war):
                if _war is not None:
                    sync.wait_ge(s_ve, _war)
                sync.dma_start(out=gb[:, TL * _slot:TL * (_slot + 1)],
                               in_=gtr[NP * _ic:NP * (_ic + 1), :]).then_inc(lsem[_slot], 16)
                c_l[_slot].add(16)
            ld_c = (lsem[slot], c_l[slot].v)

            @block.vector
            def _(ve, _slot=slot, _ld=ld_c, _ic=ic, _wk=st_done.get(ic - 2)):
                def ss(i):
                    i.then_inc(s_ve, 1)
                    ve.wait_ge(s_ve, c_ve.add(1))
                ve.wait_ge(_ld[0], _ld[1])
                if _ic == 0:
                    ve.wait_ge(s_ve, ve_bc)
                if _wk is not None:
                    ve.wait_ge(ksem[_slot], _wk)
                gv = gb[:, TL * _slot:TL * (_slot + 1)]
                ss(ve.tensor_scalar(tA[:], gv, sc[:, 2:3], None, ALU.mult))
                emit_round4(ve, ss, tA[:], tI[:], tD[:])
                ss(ve.tensor_scalar(kg_b[:, TL * _slot:TL * (_slot + 1)], tA[:],
                                    1.0, -1.0, ALU.mult, ALU.add))
                ss(ve.tensor_scalar(tA[:], tA[:], sc[:, 0:1], sc[:, 1:2], ALU.mult, ALU.add))
            ve_gq = c_ve.v

            def fsg(se, _ve=ve_gq):
                se.wait_ge(s_ve, _ve)
                se.activation(tB[:], tA[:], AF.Sigmoid).then_inc(s_se, 1)
            sq3.append(fsg)
            c_se.add(1)
            se_sg = c_se.v

            @block.vector
            def _(ve, _slot=slot, _se=se_sg):
                def ss(i):
                    i.then_inc(s_ve, 1)
                    ve.wait_ge(s_ve, c_ve.add(1))
                ve.wait_ge(s_se, _se)
                ss(ve.tensor_scalar(tB[:], tB[:], 65536.0, None, ALU.mult))
                emit_round4(ve, ss, tB[:], tI[:], tD[:])
                ss(ve.tensor_scalar(tB[:], tB[:], 65536.0, None, ALU.min))
                ss(ve.tensor_scalar(ks_b[:, TL * _slot:TL * (_slot + 1)], tB[:],
                                    1.0, -1.0, ALU.mult, ALU.add))
                ss(ve.scalar_tensor_tensor(tB[:], tB[:], -1.0, tA[:], ALU.add, ALU.mult))
                ss(ve.reduce_max(macc[:, 3:4], tB[:], axis=AX.X, apply_absolute_value=True))
                ss(ve.tensor_tensor(macc[:, 0:1], macc[:, 0:1], macc[:, 3:4], ALU.max))
            ve_done[ic] = c_ve.v

            @block.sync
            def _(sync, _ic=ic, _slot=slot, _ve=ve_done[ic]):
                sync.wait_ge(s_ve, _ve)
                sync.dma_start(out=kg_o[NP * _ic:NP * (_ic + 1), :],
                               in_=kg_b[:, TL * _slot:TL * (_slot + 1)]).then_inc(ksem[_slot], 16)
                sync.dma_start(out=ks_o[NP * _ic:NP * (_ic + 1), :],
                               in_=ks_b[:, TL * _slot:TL * (_slot + 1)]).then_inc(ksem[_slot], 16)
                c_k[_slot].add(32)
            st_done[ic] = c_k[slot].v

        @block.scalar
        def _(se, _sq=tuple(sq3)):
            for f in _sq:
                f(se)

        ve_all = c_ve.v
        with nc.psum_tensor("psT3b", [1, NP], F32) as psT:
            @block.tensor
            def _(te):
                te.wait_ge(s_ve, ve_all)
                te.transpose(psT[:], macc[:, 0:1], id128[:]).then_inc(s_pe, 1)
                c_pe.add(1)
            pe_t = c_pe.v

            @block.vector
            def _(ve):
                ve.wait_ge(s_pe, pe_t)
                ve.reduce_max(fin[:, 0:1], psT[:], axis=AX.X).then_inc(s_ve, 1)
                c_ve.add(1)
            ve_f = c_ve.v

        @block.sync
        def _(sync):
            sync.wait_ge(s_ve, ve_f)
            sync.wait_ge(s_k0, c_k[0].v)
            sync.wait_ge(s_k1, c_k[1].v)
            sync.dma_start(out=mp, in_=fin[:, 0:1]).then_inc(s_st, 16)
            c_st.add(16)
            sync.wait_ge(s_st, c_st.v)
    return nc


def build_l1_q4b():
    nc = bass.Bass("TRN2", target_bir_lowering=False)
    upr = nc.dram_tensor("upr", [3 * D, TL], F32, kind="ExternalInput").ap()
    kg_i = nc.dram_tensor("kg_i", [3 * D, TL], I16, kind="ExternalInput").ap()
    ks_i = nc.dram_tensor("ks_i", [3 * D, TL], U16, kind="ExternalInput").ap()
    mg = nc.dram_tensor("mg", [1, 8], F32, kind="ExternalInput").ap()
    mu = nc.dram_tensor("mu", [1, 8], F32, kind="ExternalInput").ap()
    mp1 = nc.dram_tensor("mp1", [1, 8], F32, kind="ExternalInput").ap()
    mp = nc.dram_tensor("mp", [1, 1], F32, kind="ExternalOutput").ap()
    ku_o = nc.dram_tensor("ku_o", [3 * D, TL], I16, kind="ExternalOutput").ap()
    k3_o = nc.dram_tensor("k3_o", [3 * D, TL], I16, kind="ExternalOutput").ap()
    NI = 3 * D // NP

    with ExitStack() as ctx:
        E = ctx.enter_context
        ub = E(nc.sbuf_tensor("ub", [NP, 2 * TL], F32))
        kgb = E(nc.sbuf_tensor("kgb", [NP, 2 * TL], I16))
        ksb = E(nc.sbuf_tensor("ksb", [NP, 2 * TL], U16))
        tA = E(nc.sbuf_tensor("tA", [NP, 2 * TL], F32))
        tB = E(nc.sbuf_tensor("tB", [NP, 2 * TL], F32))
        tD = E(nc.sbuf_tensor("tD", [NP, 2 * TL], F32))
        tI = E(nc.sbuf_tensor("tI", [NP, 2 * TL], I32))
        ku_b = E(nc.sbuf_tensor("ku_b", [NP, 2 * TL], I16))
        k3_b = E(nc.sbuf_tensor("k3_b", [NP, 2 * TL], I16))
        macc = E(nc.sbuf_tensor("macc", [NP, 4], F32))
        id128 = E(nc.sbuf_tensor("id128", [NP, NP], F32))
        id128i = E(nc.sbuf_tensor("id128i", [NP, NP], I32))
        fin = E(nc.sbuf_tensor("fin", [1, 48], F32))
        sc = E(nc.sbuf_tensor("sc", [NP, 10], F32))
        ones_1p = E(nc.sbuf_tensor("o1p", [1, NP], F32))
        mone = E(nc.sbuf_tensor("mone", [NP, 1], F32))
        s_l0 = E(nc.semaphore("s_l0"))
        s_l1 = E(nc.semaphore("s_l1"))
        s_lm = E(nc.semaphore("s_lm"))
        s_pe = E(nc.semaphore("s_pe"))
        s_se = E(nc.semaphore("s_se"))
        s_ve = E(nc.semaphore("s_ve"))
        s_gp = E(nc.semaphore("s_gp"))
        s_k0 = E(nc.semaphore("s_k0"))
        s_k1 = E(nc.semaphore("s_k1"))
        s_st = E(nc.semaphore("s_st"))
        block = E(nc.Block())
        c_pe, c_se, c_ve, c_gp, c_st, c_lm = (Ctr() for _ in range(6))
        c_l = [Ctr(), Ctr()]
        c_k = [Ctr(), Ctr()]
        lsem = [s_l0, s_l1]
        ksem = [s_k0, s_k1]

        @block.sync
        def _(sync):
            for (o, ap) in [(0, mg), (8, mu), (16, mp1)]:
                sync.dma_start(out=fin[:, o:o + 8], in_=ap).then_inc(s_lm, 16)
                c_lm.add(16)
        ld_m = c_lm.v

        @block.gpsimd
        def _(g):
            g.iota(id128i[:], pattern=[[-1, NP]], base=0, channel_multiplier=1).then_inc(s_gp, 1)
            c_gp.add(1)

        @block.vector
        def _(ve):
            def ss(i):
                i.then_inc(s_ve, 1)
                ve.wait_ge(s_ve, c_ve.add(1))
            ve.wait_ge(s_gp, c_gp.v)
            ve.wait_ge(s_lm, ld_m)
            ss(ve.memset(ones_1p[:], 1.0))
            ss(ve.memset(macc[:], 0.0))
            ss(ve.memset(mone[:], -1.0))
            ss(ve.tensor_copy(id128[:], id128i[:]))
            ss(ve.tensor_single_scalar(id128[:], id128[:], 0.0, ALU.is_equal))
            ss(ve.reduce_max(fin[:, 40:41], fin[:, 0:8], axis=AX.X))
            ss(ve.tensor_scalar(fin[:, 24:25], fin[:, 40:41], 1.0 / 32767.0, 1e-12,
                                ALU.mult, ALU.max))
            ss(ve.reduce_max(fin[:, 40:41], fin[:, 8:16], axis=AX.X))
            ss(ve.tensor_scalar(fin[:, 25:26], fin[:, 40:41], 1.0 / 32767.0, 1e-12,
                                ALU.mult, ALU.max))
            ss(ve.tensor_scalar(fin[:, 26:27], fin[:, 25:26], -1.0, None, ALU.mult))
            ss(ve.reciprocal(fin[:, 27:28], fin[:, 25:26]))
            ss(ve.reduce_max(fin[:, 40:41], fin[:, 16:24], axis=AX.X))
            ss(ve.tensor_scalar(fin[:, 40:41], fin[:, 40:41], 1.0 / 65536.0, None, ALU.mult))
            ss(ve.tensor_scalar(fin[:, 28:29], fin[:, 40:41], 1.0 / 32767.0, 1e-12,
                                ALU.mult, ALU.max))
            ss(ve.tensor_scalar(fin[:, 29:30], fin[:, 28:29], -1.0, None, ALU.mult))
            ss(ve.tensor_scalar(fin[:, 30:31], fin[:, 28:29], 65536.0, None, ALU.mult))
            ss(ve.reciprocal(fin[:, 30:31], fin[:, 30:31]))
        ve_sc = c_ve.v

        with nc.psum_tensor("psB4b", [NP, 7], F32) as psB:
            @block.tensor
            def _(te):
                te.wait_ge(s_ve, ve_sc)
                te.matmul(psB[:], ones_1p[:], fin[:, 24:31], start=True, stop=True).then_inc(s_pe, 1)
                c_pe.add(1)
            pe_b = c_pe.v

            @block.vector
            def _(ve):
                def ss(i):
                    i.then_inc(s_ve, 1)
                    ve.wait_ge(s_ve, c_ve.add(1))
                ve.wait_ge(s_pe, pe_b)
                for j in range(7):
                    ss(ve.tensor_copy(sc[:, j:j + 1], psB[:, j:j + 1]))
        ve_bc = c_ve.v

        ve_done = {}
        st_done = {}
        se_done = {}
        sq_all = []
        for ic in range(NI):
            slot = ic % 2
            war = ve_done.get(ic - 2)
            war_se = se_done.get(ic - 2)

            @block.sync
            def _(sync, _ic=ic, _slot=slot, _war=war, _wse=war_se):
                if _war is not None:
                    sync.wait_ge(s_ve, _war)
                if _wse is not None:
                    sync.wait_ge(s_se, _wse)
                sync.dma_start(out=ub[:, TL * _slot:TL * (_slot + 1)],
                               in_=upr[NP * _ic:NP * (_ic + 1), :]).then_inc(lsem[_slot], 16)
                sync.dma_start(out=kgb[:, TL * _slot:TL * (_slot + 1)],
                               in_=kg_i[NP * _ic:NP * (_ic + 1), :]).then_inc(lsem[_slot], 16)
                sync.dma_start(out=ksb[:, TL * _slot:TL * (_slot + 1)],
                               in_=ks_i[NP * _ic:NP * (_ic + 1), :]).then_inc(lsem[_slot], 16)
                c_l[_slot].add(48)
            ld_c = (lsem[slot], c_l[slot].v)

            uv = ub[:, TL * slot:TL * (slot + 1)]
            kgv = kgb[:, TL * slot:TL * (slot + 1)]
            ksv = ksb[:, TL * slot:TL * (slot + 1)]
            tAs = tA[:, TL * slot:TL * (slot + 1)]
            tBs = tB[:, TL * slot:TL * (slot + 1)]
            tDs = tD[:, TL * slot:TL * (slot + 1)]
            tIs = tI[:, TL * slot:TL * (slot + 1)]

            def S(fn):
                sq_all.append(fn)
                c_se.add(1)

            def s1(se, _kgv=kgv, _ld=ld_c, _ic=ic, _wk=st_done.get(ic - 2),
                   _vprev=ve_done.get(ic - 2), _tA=tAs):
                se.wait_ge(_ld[0], _ld[1])
                if _ic == 0:
                    se.wait_ge(s_ve, ve_bc)
                if _wk is not None:
                    se.wait_ge(ksem[_ic % 2], _wk)
                if _vprev is not None:
                    se.wait_ge(s_ve, _vprev)
                se.activation(_tA, _kgv, AF.Copy,
                              scale=sc[:, 0:1]).then_inc(s_se, 1)
            S(s1)
            se_s1 = c_se.v

            def s3(se, _uv=uv, _tB=tBs):
                se.activation(_tB, _uv, AF.Copy,
                              scale=sc[:, 3:4]).then_inc(s_se, 1)
            S(s3)
            se_s3 = c_se.v

            @block.vector
            def _(ve, _se=se_s1, _ksv=ksv, _tA=tAs):
                ve.wait_ge(s_se, _se)
                ve.tensor_tensor(_tA, _tA, _ksv, ALU.mult).then_inc(s_ve, 1)
                c_ve.add(1)
            ve_p1 = c_ve.v

            @block.vector
            def _(ve, _se=se_s3, _tB=tBs, _tI=tIs, _tD=tDs):
                def ss(i):
                    i.then_inc(s_ve, 1)
                    ve.wait_ge(s_ve, c_ve.add(1))
                ve.wait_ge(s_se, _se)
                emit_round4(ve, ss, _tB, _tI, _tD)
                ss(ve.tensor_scalar(_tB, _tB, 1.0, -1.0, ALU.mult, ALU.add))
            ve_b1 = c_ve.v

            def s9(se, _ve=ve_b1, _slot=slot, _tB=tBs):
                se.wait_ge(s_ve, _ve)
                se.activation(ku_b[:, TL * _slot:TL * (_slot + 1)], _tB,
                              AF.Copy).then_inc(s_se, 1)
            S(s9)
            se_s9 = c_se.v

            def s11(se, _ve=ve_p1, _tA=tAs):
                se.wait_ge(s_ve, _ve)
                se.activation(_tA, _tA, AF.Copy,
                              scale=sc[:, 6:7]).then_inc(s_se, 1)
            S(s11)
            se_s11 = c_se.v

            @block.vector
            def _(ve, _se=se_s11, _tA=tAs, _tI=tIs, _tD=tDs):
                def ss(i):
                    i.then_inc(s_ve, 1)
                    ve.wait_ge(s_ve, c_ve.add(1))
                ve.wait_ge(s_se, _se)
                emit_round4(ve, ss, _tA, _tI, _tD)
                ss(ve.tensor_scalar(_tA, _tA, 1.0, -1.0, ALU.mult, ALU.add))
            ve_rA = c_ve.v

            def s17(se, _ve=ve_rA, _slot=slot, _tA=tAs):
                se.wait_ge(s_ve, _ve)
                se.activation(k3_b[:, TL * _slot:TL * (_slot + 1)], _tA,
                              AF.Copy).then_inc(s_se, 1)
            S(s17)
            se_s17 = c_se.v
            se_done[ic] = c_se.v

            @block.vector
            def _(ve, _se=se_s17, _tA=tAs, _tB=tBs):
                def ss(i):
                    i.then_inc(s_ve, 1)
                    ve.wait_ge(s_ve, c_ve.add(1))
                ve.wait_ge(s_se, _se)
                ss(ve.tensor_tensor(_tB, _tA, _tB, ALU.mult))
                ss(ve.reduce_max(macc[:, 3:4], _tB, axis=AX.X,
                                 apply_absolute_value=True))
                ss(ve.tensor_tensor(macc[:, 0:1], macc[:, 0:1], macc[:, 3:4], ALU.max))
            ve_done[ic] = c_ve.v

            @block.sync
            def _(sync, _ic=ic, _slot=slot, _se=se_s17):
                sync.wait_ge(s_se, _se)
                sync.dma_start(out=ku_o[NP * _ic:NP * (_ic + 1), :],
                               in_=ku_b[:, TL * _slot:TL * (_slot + 1)]).then_inc(ksem[_slot], 16)
                sync.dma_start(out=k3_o[NP * _ic:NP * (_ic + 1), :],
                               in_=k3_b[:, TL * _slot:TL * (_slot + 1)]).then_inc(ksem[_slot], 16)
                c_k[_slot].add(32)
            st_done[ic] = c_k[slot].v

        @block.scalar
        def _(se, _sq=tuple(sq_all)):
            for f in _sq:
                f(se)

        ve_all = c_ve.v
        with nc.psum_tensor("psT4b", [1, NP], F32) as psT:
            @block.tensor
            def _(te):
                te.wait_ge(s_ve, ve_all)
                te.transpose(psT[:], macc[:, 0:1], id128[:]).then_inc(s_pe, 1)
                c_pe.add(1)
            pe_t = c_pe.v

            @block.vector
            def _(ve):
                def ss(i):
                    i.then_inc(s_ve, 1)
                    ve.wait_ge(s_ve, c_ve.add(1))
                ve.wait_ge(s_pe, pe_t)
                ss(ve.reduce_max(fin[:, 0:1], psT[:], axis=AX.X))
                ss(ve.tensor_tensor(fin[:, 0:1], fin[:, 0:1], fin[:, 28:29], ALU.mult))
                ss(ve.tensor_tensor(fin[:, 0:1], fin[:, 0:1], fin[:, 25:26], ALU.mult))
            ve_f = c_ve.v

        @block.sync
        def _(sync):
            sync.wait_ge(s_ve, ve_f)
            sync.wait_ge(s_k0, c_k[0].v)
            sync.wait_ge(s_k1, c_k[1].v)
            sync.dma_start(out=mp, in_=fin[:, 0:1]).then_inc(s_st, 16)
            c_st.add(16)
            sync.wait_ge(s_st, c_st.v)
    return nc


def build_l1_q5b():
    nc = bass.Bass("TRN2", target_bir_lowering=False)
    x3 = nc.dram_tensor("x3", [D, TL], F32, kind="ExternalInput").ap()
    k3_i = nc.dram_tensor("k3_i", [3 * D, TL], I16, kind="ExternalInput").ap()
    ku_i = nc.dram_tensor("ku_i", [3 * D, TL], I16, kind="ExternalInput").ap()
    mg = nc.dram_tensor("mg", [1, 8], F32, kind="ExternalInput").ap()
    mu = nc.dram_tensor("mu", [1, 8], F32, kind="ExternalInput").ap()
    mp1 = nc.dram_tensor("mp1", [1, 8], F32, kind="ExternalInput").ap()
    mp2 = nc.dram_tensor("mp2", [1, 8], F32, kind="ExternalInput").ap()
    wdn = nc.dram_tensor("wdn", [D, 3 * D], F32, kind="ExternalInput").ap()
    x4 = nc.dram_tensor("x4", [D, TL], F32, kind="ExternalOutput").ap()
    wqd_dram = nc.dram_tensor("wqd_dram", [D, 3 * D], BF16).ap()
    NI = 3 * D // NP

    with ExitStack() as ctx:
        E = ctx.enter_context
        gb = E(nc.sbuf_tensor("gb", [NP, 2 * 512], I16))
        ub = E(nc.sbuf_tensor("ub", [NP, 2 * 512], I16))
        x3b = E(nc.sbuf_tensor("x3b", [NP, 8 * 512], F32))
        tA = E(nc.sbuf_tensor("tA", [NP, 3 * D], F32))
        tB = E(nc.sbuf_tensor("tB", [NP, 3 * D], F32))
        tC = E(nc.sbuf_tensor("tC", [NP, 3 * D], F32))
        tD = E(nc.sbuf_tensor("tD", [NP, 3 * D], F32))
        tI = E(nc.sbuf_tensor("tI", [NP, 3 * D], I32))
        wbuf = E(nc.sbuf_tensor("wbuf", [NP, 2 * 3 * D], F32))
        wsc = E(nc.sbuf_tensor("wsc", [NP, 3 * 96], F32))
        wdT = E(nc.sbuf_tensor("wdT", [NP, NI * D], BF16))
        wqb5 = E(nc.sbuf_tensor("wqb5", [NP, 2 * 3 * D], BF16))
        oq = E(nc.sbuf_tensor("oq", [NP, NI * 512], BF16))
        ev = E(nc.sbuf_tensor("ev", [NP, 2 * 512], F32))
        fin = E(nc.sbuf_tensor("fin", [1, 48], F32))
        sc = E(nc.sbuf_tensor("sc", [NP, 10], F32))
        ones_1p = E(nc.sbuf_tensor("o1p", [1, NP], F32))
        s_lm = E(nc.semaphore("s_lm"))
        s_lw0 = E(nc.semaphore("s_lw0"))
        s_lw1 = E(nc.semaphore("s_lw1"))
        s_wq = E(nc.semaphore("s_wq"))
        s_tq = E(nc.semaphore("s_tq"))
        s_l0 = E(nc.semaphore("s_l0"))
        s_l1 = E(nc.semaphore("s_l1"))
        s_lx = E(nc.semaphore("s_lx"))
        s_pe = E(nc.semaphore("s_pe"))
        s_se = E(nc.semaphore("s_se"))
        s_ve = E(nc.semaphore("s_ve"))
        s_st0 = E(nc.semaphore("s_st0"))
        s_st1 = E(nc.semaphore("s_st1"))
        block = E(nc.Block())
        c_lm, c_wq, c_tq, c_lx, c_pe, c_se, c_ve = (Ctr() for _ in range(7))
        c_lw = [Ctr(), Ctr()]
        c_l = [Ctr(), Ctr()]
        c_sto = [Ctr(), Ctr()]
        lwsem = [s_lw0, s_lw1]
        lsem = [s_l0, s_l1]
        stsem = [s_st0, s_st1]

        @block.sync
        def _(sync):
            for (o, ap) in [(0, mg), (8, mu), (16, mp1), (24, mp2)]:
                sync.dma_start(out=fin[:, o:o + 8], in_=ap).then_inc(s_lm, 16)
                c_lm.add(16)
        ld_m = c_lm.v

        sc4 = E(nc.sbuf_tensor("sc4", [NP, 3], F32))

        @block.vector
        def _(ve):
            def ss(i):
                i.then_inc(s_ve, 1)
                ve.wait_ge(s_ve, c_ve.add(1))
            ve.wait_ge(s_lm, ld_m)
            ss(ve.memset(ones_1p[:], 1.0))
            ss(ve.reduce_max(fin[:, 40:41], fin[:, 0:8], axis=AX.X))
            ss(ve.tensor_scalar(fin[:, 32:33], fin[:, 40:41], 1.0 / 32767.0, 1e-12,
                                ALU.mult, ALU.max))
            ss(ve.reciprocal(fin[:, 33:34], fin[:, 32:33]))
            ss(ve.reduce_max(fin[:, 40:41], fin[:, 8:16], axis=AX.X))
            ss(ve.tensor_scalar(fin[:, 34:35], fin[:, 40:41], 1.0 / 32767.0, 1e-12,
                                ALU.mult, ALU.max))
            ss(ve.reciprocal(fin[:, 35:36], fin[:, 34:35]))
            ss(ve.reduce_max(fin[:, 40:41], fin[:, 16:24], axis=AX.X))
            ss(ve.tensor_scalar(fin[:, 40:41], fin[:, 40:41], 1.0 / 65536.0, None, ALU.mult))
            ss(ve.tensor_scalar(fin[:, 36:37], fin[:, 40:41], 1.0 / 32767.0, 1e-12,
                                ALU.mult, ALU.max))
            ss(ve.tensor_scalar(fin[:, 37:38], fin[:, 36:37], 65536.0, None, ALU.mult))
            ss(ve.reciprocal(fin[:, 37:38], fin[:, 37:38]))
            ss(ve.reduce_max(fin[:, 40:41], fin[:, 24:32], axis=AX.X))
            ss(ve.tensor_scalar(fin[:, 38:39], fin[:, 40:41], 1.0 / 32767.0, 1e-12,
                                ALU.mult, ALU.max))
            ss(ve.reciprocal(fin[:, 39:40], fin[:, 38:39]))
        ve_sc = c_ve.v

        with nc.psum_tensor("psB5", [NP, 8], F32) as psB:
            @block.tensor
            def _(te):
                te.wait_ge(s_ve, ve_sc)
                te.matmul(psB[:], ones_1p[:], fin[:, 32:40], start=True, stop=True).then_inc(s_pe, 1)
                c_pe.add(1)
            pe_b = c_pe.v

            @block.vector
            def _(ve):
                def ss(i):
                    i.then_inc(s_ve, 1)
                    ve.wait_ge(s_ve, c_ve.add(1))
                ve.wait_ge(s_pe, pe_b)
                ss(ve.tensor_copy(sc[:, 0:1], psB[:, 0:1]))
                ss(ve.tensor_scalar(sc[:, 1:2], psB[:, 0:1], -1.0, None, ALU.mult))
                ss(ve.tensor_copy(sc[:, 2:3], psB[:, 1:2]))
                ss(ve.tensor_copy(sc[:, 6:7], psB[:, 2:3]))
                ss(ve.tensor_scalar(sc[:, 7:8], psB[:, 2:3], -1.0, None, ALU.mult))
                ss(ve.tensor_copy(sc[:, 8:9], psB[:, 3:4]))
                ss(ve.tensor_copy(sc[:, 3:4], psB[:, 4:5]))
                ss(ve.tensor_scalar(sc[:, 4:5], psB[:, 4:5], -1.0, None, ALU.mult))
                ss(ve.tensor_copy(sc[:, 5:6], psB[:, 5:6]))
                ss(ve.tensor_copy(sc4[:, 0:1], psB[:, 6:7]))
                ss(ve.tensor_scalar(sc4[:, 1:2], psB[:, 6:7], -1.0, None, ALU.mult))
                ss(ve.tensor_copy(sc4[:, 2:3], psB[:, 7:8]))
        ve_bc = c_ve.v

        ve_dq_at = {}
        for r in range(8):
            slot = r % 2
            war = ve_dq_at.get(r - 2)

            @block.sync
            def _(sync, _r=r, _slot=slot, _war=war):
                if _war is not None:
                    sync.wait_ge(s_ve, _war)
                sync.dma_start(out=wbuf[:, 3 * D * _slot:3 * D * (_slot + 1)],
                               in_=wdn[NP * _r:NP * (_r + 1), :]).then_inc(lwsem[_slot], 16)
                c_lw[_slot].add(16)
            ld_w = (lwsem[slot], c_lw[slot].v)

            @block.vector
            def _(ve, _r=r, _slot=slot, _ld=ld_w, _wq=(c_wq.v if r >= 2 else None)):
                def ss(i):
                    i.then_inc(s_ve, 1)
                    ve.wait_ge(s_ve, c_ve.add(1))
                ve.wait_ge(_ld[0], _ld[1])
                if _wq is not None:
                    ve.wait_ge(s_wq, _wq)
                if _r == 0:
                    ve.wait_ge(s_ve, ve_bc)
                wv = wbuf[:, 3 * D * _slot:3 * D * (_slot + 1)]
                w3 = wv.rearrange("p (a b) -> p a b", b=32)
                mx = wsc[:, 0:96]; sc_ = wsc[:, 96:192]; rc = wsc[:, 192:288]
                ss(ve.reduce_max(mx, w3, axis=AX.X, apply_absolute_value=True))
                ss(ve.tensor_scalar(sc_, mx, 1.0 / 7.0, 1e-12, ALU.mult, ALU.max))
                ss(ve.reciprocal(rc, sc_))
                rc3 = rc.unsqueeze(2).broadcast_to([NP, 96, 32])
                sc3 = sc_.unsqueeze(2).broadcast_to([NP, 96, 32])
                y3 = tA[:].rearrange("p (a b) -> p a b", b=32)
                ss(ve.tensor_tensor(y3, w3, rc3, ALU.mult))
                emit_round_chain(ve, ss, tA[:], tI[:], tB[:], tC[:])
                r3 = tB[:].rearrange("p (a b) -> p a b", b=32)
                u3 = tC[:].rearrange("p (a b) -> p a b", b=32)
                ss(ve.tensor_tensor(u3, r3, sc3, ALU.mult))
                q3 = wqb5[:, 3 * D * _slot:3 * D * (_slot + 1)].rearrange("p (a b) -> p a b", b=32)
                ss(ve.tensor_tensor(q3, u3, sc3, ALU.subtract))
            ve_dq_at[r] = c_ve.v

            @block.sync
            def _(sync, _r=r, _slot=slot, _ve=ve_dq_at[r]):
                sync.wait_ge(s_ve, _ve)
                sync.dma_start(out=wqd_dram[NP * _r:NP * (_r + 1), :],
                               in_=wqb5[:, 3 * D * _slot:3 * D * (_slot + 1)]).then_inc(s_wq, 16)
                c_wq.add(16)
        wq_done = c_wq.v

        @block.sync
        def _(sync):
            sync.wait_ge(s_wq, wq_done)
            for ic in range(NI):
                sync.dma_start(out=wdT[:, D * ic:D * (ic + 1)],
                               in_=wqd_dram[:, NP * ic:NP * (ic + 1)],
                               transpose=True).then_inc(s_tq, 16)
                c_tq.add(16)
        tq_done = c_tq.v


        ve_oq_prev = 0
        st_prev5 = {}
        ve_d_at = {}
        psD_ctx = nc.psum_tensor("psD", [NP, 2 * 512], F32)
        psD = psD_ctx.__enter__()
        for tt in range(4):
            @block.sync
            def _(sync, _tt=tt, _vp=ve_oq_prev):
                if _vp:
                    sync.wait_ge(s_ve, _vp)
                for c in range(8):
                    sync.dma_start(out=x3b[:, 512 * c:512 * (c + 1)],
                                   in_=x3[NP * c:NP * (c + 1), 512 * _tt:512 * (_tt + 1)]).then_inc(
                                       s_lx, 16)
                    c_lx.add(16)
            ld_x = c_lx.v

            ve_oq_at = {}
            for ic in range(NI):
                slot = ic % 2
                war = ve_oq_at.get(ic - 2, ve_oq_prev if ic < 2 else None)

                @block.sync
                def _(sync, _ic=ic, _tt=tt, _slot=slot, _war=war):
                    if _war is not None:
                        sync.wait_ge(s_ve, _war)
                    sync.dma_start(out=gb[:, 512 * _slot:512 * (_slot + 1)],
                                   in_=k3_i[NP * _ic:NP * (_ic + 1), 512 * _tt:512 * (_tt + 1)]).then_inc(
                                       lsem[_slot], 16)
                    sync.dma_start(out=ub[:, 512 * _slot:512 * (_slot + 1)],
                                   in_=ku_i[NP * _ic:NP * (_ic + 1), 512 * _tt:512 * (_tt + 1)]).then_inc(
                                       lsem[_slot], 16)
                    c_l[_slot].add(32)
                ld_c = (lsem[slot], c_l[slot].v)

                @block.vector
                def _(ve, _ic=ic, _slot=slot, _ld=ld_c, _tt=tt):
                    def ss(i):
                        i.then_inc(s_ve, 1)
                        ve.wait_ge(s_ve, c_ve.add(1))
                    ve.wait_ge(_ld[0], _ld[1])
                    if _tt == 0 and _ic == 0:
                        ve.wait_ge(s_tq, tq_done)
                    A = tA[:, 0:512]; B = tB[:, 0:512]; C = tC[:, 0:512]; Dd = tD[:, 0:512]
                    Ii = tI[:, 0:512]
                    gv = gb[:, 512 * _slot:512 * (_slot + 1)]
                    uv = ub[:, 512 * _slot:512 * (_slot + 1)]
                    ss(ve.tensor_scalar(A, gv, sc[:, 3:4], None, ALU.mult))
                    ss(ve.tensor_scalar(B, uv, sc[:, 6:7], None, ALU.mult))
                    ss(ve.tensor_tensor(B, A, B, ALU.mult))
                    ss(ve.tensor_scalar(A, B, sc4[:, 2:3], None, ALU.mult))
                    emit_round4(ve, ss, A, Ii, C)
                    ss(ve.tensor_scalar(oq[:, 512 * _ic:512 * (_ic + 1)], A,
                                        sc4[:, 0:1], sc4[:, 1:2], ALU.mult, ALU.add))
                ve_oq_at[ic] = c_ve.v
            ve_oq_prev = c_ve.v

            for dc in range(8):
                job5 = tt * 8 + dc
                par = job5 % 2

                @block.tensor
                def _(te, _dc=dc, _tt=tt, _par=par, _job=job5, _vq=ve_oq_prev):
                    if _dc == 0:
                        te.wait_ge(s_ve, _vq)
                        te.wait_ge(s_lx, ld_x)
                    if _job >= 2:
                        te.wait_ge(s_ve, ve_d_at[_job - 2])
                    for ic in range(NI):
                        i = te.matmul(psD[:, 512 * _par:512 * (_par + 1)],
                                      wdT[:, D * ic + NP * _dc:D * ic + NP * (_dc + 1)],
                                      oq[:, 512 * ic:512 * (ic + 1)],
                                      start=(ic == 0), stop=(ic == NI - 1))
                    i.then_inc(s_pe, 1)
                    c_pe.add(1)
                pe_d = c_pe.v

                @block.vector
                def _(ve, _dc=dc, _par=par, _pe=pe_d, _war=st_prev5.get(par)):
                    ve.wait_ge(s_pe, _pe)
                    if _war is not None:
                        ve.wait_ge(stsem[_par], _war)
                    ve.tensor_tensor(ev[:, 512 * _par:512 * (_par + 1)],
                                     x3b[:, 512 * _dc:512 * (_dc + 1)],
                                     psD[:, 512 * _par:512 * (_par + 1)],
                                     ALU.add).then_inc(s_ve, 1)
                    c_ve.add(1)
                ve_d_at[job5] = c_ve.v

                @block.sync
                def _(sync, _dc=dc, _tt=tt, _par=par, _ve=ve_d_at[job5]):
                    sync.wait_ge(s_ve, _ve)
                    sync.dma_start(out=x4[NP * _dc:NP * (_dc + 1), 512 * _tt:512 * (_tt + 1)],
                                   in_=ev[:, 512 * _par:512 * (_par + 1)]).then_inc(
                                       stsem[_par], 16)
                    c_sto[_par].add(16)
                st_prev5[par] = c_sto[par].v

        psD_ctx.__exit__(None, None, None)

        @block.sync
        def _(sync):
            sync.wait_ge(s_st0, c_sto[0].v)
            sync.wait_ge(s_st1, c_sto[1].v)
    return nc



